# version 01
# baseline (speedup 1.0000x reference)
import numpy as np
import ml_dtypes
from contextlib import ExitStack
import concourse.bass as bass
import concourse.mybir as mybir
from concourse.bass_utils import run_bass_kernel_spmd

F32 = mybir.dt.float32
BF16 = mybir.dt.bfloat16
U8 = mybir.dt.uint8
AF = mybir.ActivationFunctionType
ALU = mybir.AluOpType

BATCH, T, DM = 4, 4096, 1024
TH = T // 2
DD = 2048
NST, DTR, CONVK = 16, 64, 4
HFF = 4096
HEADS, DHEAD, INNER = 16, 64, 1024
NEXP, IEXP, ISH = 16, 512, 2048
BETA = 0.125
EPS = 1e-5

C_ONE, C_NEG1, C_EPS, C_BETA, C_INV1024, C_SCAL25, C_ZERO = range(7)
NC = 7


def mknc():
    return bass.Bass("TRN2", target_bir_lowering=False, detect_race_conditions=False)


def fm(a):
    d, t = a.shape
    return np.ascontiguousarray(a.reshape(d // 128, 128, t).swapaxes(0, 1))


def unfm(a):
    p, c, t = a.shape
    return np.ascontiguousarray(a.swapaxes(0, 1).reshape(c * p, t))


def cols(v):
    return np.ascontiguousarray(v.reshape(-1, 128).T)


def host_consts():
    cc = np.zeros((128, NC), np.float32)
    cc[:, C_ONE] = 1.0
    cc[:, C_NEG1] = -1.0
    cc[:, C_EPS] = EPS
    cc[:, C_BETA] = BETA
    cc[:, C_INV1024] = 1.0 / 1024.0
    cc[:, C_SCAL25] = 2.5
    return cc


class Prog:
    ENGINES = ("sync", "gpsimd", "tensor", "vector", "scalar")

    def __init__(self, nc, ctx):
        self.nc = nc
        self.ctx = ctx
        self.ops = {e: [] for e in self.ENGINES}
        self._semv = {}

    def sem(self, name):
        h = self.ctx.enter_context(self.nc.semaphore(name))
        self._semv[name] = 0
        return h

    def bump(self, h, n=1):
        self._semv[h.name] += n
        return self._semv[h.name]

    def val(self, h):
        return self._semv[h.name]

    def on(self, eng, fn):
        self.ops[eng].append(fn)

    def wait(self, eng, h, v):
        if v > 0:
            self.ops[eng].append(lambda e, h=h, v=v: e.wait_ge(h, v))

    def finalize(self, block):
        for name in self.ENGINES:
            ops = self.ops[name]
            if not ops:
                continue
            def run(e, ops=ops):
                for fn in ops:
                    fn(e)
            getattr(block, name)(run)


class RR:

    def __init__(self, p, banks):
        self.p = p
        self.banks = banks
        self.pending = []
        self.acq = 0

    def acquire(self):
        k = self.acq % len(self.banks)
        w = []
        back = self.acq - len(self.banks)
        if back >= 0:
            sv = self.pending[back]
            assert sv is not None, "bank reacquired before release registered"
            w.append(sv)
        self.pending.append(None)
        self.acq += 1
        return self.banks[k], w

    def release(self, sem, val):
        for i in range(len(self.pending) - 1, -1, -1):
            if self.pending[i] is None:
                self.pending[i] = (sem, val)
                return
        raise AssertionError("release without acquire")


def sbuf(nc, ctx, name, shape, dtype):
    return ctx.enter_context(nc.sbuf_tensor(name, shape, dtype))


def psum(nc, ctx, name, shape, dtype=F32):
    return ctx.enter_context(nc.psum_tensor(name, shape, dtype))


def emit_mm_group(p, smm, bank, mats, extra_waits=(), width=None, rows=None):
    for hh, vv in extra_waits:
        p.wait("tensor", hh, vv)
    n = len(mats)
    gv = None

    def out_ap(bank=bank, width=width, rows=rows):
        a = bank[:] if rows is None else bank[0:rows, :]
        if width is not None:
            a = bank[:, 0:width] if rows is None else bank[0:rows, 0:width]
        return a

    for i, (lf, rf) in enumerate(mats):
        last = i == n - 1
        if last:
            gv = p.bump(smm)
            p.on("tensor", lambda t, lf=lf, rf=rf, i=i:
                 t.matmul(out_ap(), lf(), rf(), start=(i == 0), stop=True).then_inc(smm, 1))
        else:
            p.on("tensor", lambda t, lf=lf, rf=rf, i=i:
                 t.matmul(out_ap(), lf(), rf(), start=(i == 0), stop=False))
    return gv



def build_A():
    nc = mknc()
    TT = TH
    NT = TT // 512

    xT = nc.dram_tensor("xT", [128, 8, TT + 3], F32, kind="ExternalInput").ap()
    W1 = nc.dram_tensor("W1", [128, 8, DD], F32, kind="ExternalInput").ap()
    W2 = nc.dram_tensor("W2", [128, 8, DD], F32, kind="ExternalInput").ap()
    XPW = nc.dram_tensor("XPW", [128, 16, 96], F32, kind="ExternalInput").ap()
    DTW = nc.dram_tensor("DTW", [64, DD], F32, kind="ExternalInput").ap()
    B1 = nc.dram_tensor("B1", [128, 16], F32, kind="ExternalInput").ap()
    B2 = nc.dram_tensor("B2", [128, 16], F32, kind="ExternalInput").ap()
    CW = nc.dram_tensor("CW", [128, 16, 4], F32, kind="ExternalInput").ap()
    CB = nc.dram_tensor("CB", [128, 16], F32, kind="ExternalInput").ap()
    DTB = nc.dram_tensor("DTB", [128, 16], F32, kind="ExternalInput").ap()
    HS = nc.dram_tensor("HS", [128, 1], F32, kind="ExternalInput").ap()
    CC = nc.dram_tensor("CC", [128, NC], F32, kind="ExternalInput").ap()

    u_o = nc.dram_tensor("u_o", [128, 16, TT], BF16, kind="ExternalOutput").ap()
    gate_o = nc.dram_tensor("gate_o", [128, 16, TT], BF16, kind="ExternalOutput").ap()
    delta_o = nc.dram_tensor("delta_o", [128, 16, TT], BF16, kind="ExternalOutput").ap()
    bc_o = nc.dram_tensor("bc_o", [32, TT], F32, kind="ExternalOutput").ap()

    with ExitStack() as ctx:
        xTs = sbuf(nc, ctx, "xTs", [128, 8, TT + 3], BF16)
        Wb = sbuf(nc, ctx, "Wb", [128, 8, DD], BF16)
        xpwb = sbuf(nc, ctx, "xpwb", [128, 16, 96], BF16)
        dtwb = sbuf(nc, ctx, "dtwb", [64, DD], BF16)
        b1s = sbuf(nc, ctx, "b1s", [128, 16], F32)
        b2s = sbuf(nc, ctx, "b2s", [128, 16], F32)
        cws = sbuf(nc, ctx, "cws", [128, 16, 4], F32)
        cbs = sbuf(nc, ctx, "cbs", [128, 16], F32)
        dtbs = sbuf(nc, ctx, "dtbs", [128, 16], F32)
        hss = sbuf(nc, ctx, "hss", [128, 1], F32)
        ccs = sbuf(nc, ctx, "ccs", [128, NC], F32)

        u0 = sbuf(nc, ctx, "u0", [128, 16, 512], BF16)
        carry = sbuf(nc, ctx, "carry", [128, 16, 3], BF16)
        cvt = sbuf(nc, ctx, "cvt", [128, 16, 512], BF16)
        u_sb = sbuf(nc, ctx, "u_sb", [128, 16, TT], BF16)
        gt = sbuf(nc, ctx, "gt", [128, 16, 512], BF16)
        xp_sb = sbuf(nc, ctx, "xp_sb", [96, TT], F32)
        xp64b = sbuf(nc, ctx, "xp64b", [64, 512], BF16)
        sp1 = [sbuf(nc, ctx, f"sp1_{i}", [128, 512], BF16) for i in range(2)]
        sp2 = [sbuf(nc, ctx, f"sp2_{i}", [128, 512], BF16) for i in range(2)]
        dlt = gt

        banks = [psum(nc, ctx, f"bank{i}", [128, 512], F32) for i in range(3)]
        ps_xp = psum(nc, ctx, "ps_xp", [96, 512], F32)

        p = Prog(nc, ctx)
        ld = p.sem("ld")
        lg = p.sem("lg")
        smm = p.sem("smm")
        sac = p.sem("sac")
        sdv = p.sem("sdv")
        st = p.sem("st")
        rr = RR(p, banks)
        rrxp = RR(p, [ps_xp])

        for sb_, dr in [(xTs, xT), (Wb, W1), (xpwb, XPW), (dtwb, DTW)]:
            p.bump(lg, 16)
            p.on("gpsimd", lambda g, sb_=sb_, dr=dr: g.dma_start(out=sb_[:], in_=dr[:]).then_inc(lg, 16))
        lg_main = p.val(lg)
        for sb_, dr in [(b1s, B1), (b2s, B2), (cws, CW), (cbs, CB), (dtbs, DTB), (hss, HS), (ccs, CC)]:
            p.bump(ld, 16)
            p.on("sync", lambda s, sb_=sb_, dr=dr: s.dma_start(out=sb_[:], in_=dr[:]).then_inc(ld, 16))
        ld_small = p.val(ld)

        for j in range(16):
            bank, w = rr.acquire()
            gv = emit_mm_group(
                p, smm, bank,
                [(lambda c=c, j=j: Wb[:, c, j * 128:(j + 1) * 128],
                  lambda c=c: xTs[:, c, 0:3]) for c in range(8)],
                extra_waits=w + ([(lg, lg_main)] if j == 0 else []), width=3)
            p.wait("scalar", smm, gv)
            if j == 0:
                p.wait("scalar", ld, ld_small)
            p.on("scalar", lambda s, bank=bank, j=j:
                 s.activation(out=carry[:, j, :], in_=bank[:, 0:3], func=AF.Identity,
                              bias=b1s[:, j:j + 1]).then_inc(sac, 1))
            rr.release(sac, p.bump(sac))
        p.wait("vector", sac, p.val(sac))
        p.on("vector", lambda v: v.tensor_scalar_mul(carry[:], carry[:], hss[:, 0:1]).then_inc(sdv, 1))
        conv_done = p.bump(sdv)

        for it in range(NT):
            t0 = it * 512
            u0_last = None
            for j in range(16):
                bank, w = rr.acquire()
                gv = emit_mm_group(
                    p, smm, bank,
                    [(lambda c=c, j=j: Wb[:, c, j * 128:(j + 1) * 128],
                      lambda c=c, t0=t0: xTs[:, c, 3 + t0:3 + t0 + 512]) for c in range(8)],
                    extra_waits=w)
                p.wait("scalar", smm, gv)
                if j == 0:
                    p.wait("scalar", sdv, conv_done)
                p.on("scalar", lambda s, bank=bank, j=j:
                     s.activation(out=u0[:, j, :], in_=bank[:], func=AF.Identity,
                                  bias=b1s[:, j:j + 1]).then_inc(sac, 1))
                u0_last = p.bump(sac)
                rr.release(sac, u0_last)

            p.wait("vector", sac, u0_last)
            for j in range(16):
                p.on("vector", lambda v, j=j:
                     v.tensor_scalar(out=cvt[:, j, :], in0=u0[:, j, :],
                                     scalar1=cws[:, j, 3:4], scalar2=None, op0=ALU.mult))
                for k in range(3):
                    s_ = 3 - k
                    p.on("vector", lambda v, j=j, k=k, s_=s_:
                         v.scalar_tensor_tensor(out=cvt[:, j, 0:s_], in0=carry[:, j, k:3],
                                                scalar=cws[:, j, k:k + 1], in1=cvt[:, j, 0:s_],
                                                op0=ALU.mult, op1=ALU.add))
                    p.on("vector", lambda v, j=j, k=k, s_=s_:
                         v.scalar_tensor_tensor(out=cvt[:, j, s_:512], in0=u0[:, j, 0:512 - s_],
                                                scalar=cws[:, j, k:k + 1], in1=cvt[:, j, s_:512],
                                                op0=ALU.mult, op1=ALU.add))
            p.on("vector", lambda v: v.tensor_copy(carry[:], u0[:, :, 509:512]).then_inc(sdv, 1))
            conv_done = p.bump(sdv)
            p.wait("scalar", sdv, conv_done)
            for j in range(16):
                p.on("scalar", lambda s, j=j, t0=t0:
                     s.activation(out=u_sb[:, j, t0:t0 + 512], in_=cvt[:, j, :], func=AF.Sigmoid,
                                  bias=cbs[:, j:j + 1]).then_inc(sac, 1))
                p.bump(sac)
        u_all_done = p.val(sac)
        u0_mms_done = p.val(smm)

        p.wait("sync", sac, u_all_done)
        p.bump(st, 16)
        p.on("sync", lambda s: s.dma_start(out=u_o[:], in_=u_sb[:]).then_inc(st, 16))

        p.wait("gpsimd", smm, u0_mms_done)
        p.bump(lg, 16)
        p.on("gpsimd", lambda g: g.dma_start(out=Wb[:], in_=W2[:]).then_inc(lg, 16))
        lg_w2 = p.val(lg)
        gate_store = 0
        for it in range(NT):
            t0 = it * 512
            for j in range(16):
                bank, w = rr.acquire()
                gv = emit_mm_group(
                    p, smm, bank,
                    [(lambda c=c, j=j: Wb[:, c, j * 128:(j + 1) * 128],
                      lambda c=c, t0=t0: xTs[:, c, 3 + t0:3 + t0 + 512]) for c in range(8)],
                    extra_waits=w + ([(lg, lg_w2)] if (it == 0 and j == 0) else []))
                p.wait("scalar", smm, gv)
                if j == 0 and it > 0:
                    p.wait("scalar", st, gate_store)
                p.on("scalar", lambda s, bank=bank, j=j:
                     s.activation(out=gt[:, j, :], in_=bank[:], func=AF.Sigmoid,
                                  bias=b2s[:, j:j + 1]).then_inc(sac, 1))
                rr.release(sac, p.bump(sac))
            p.wait("sync", sac, p.val(sac))
            p.bump(st, 16)
            p.on("sync", lambda s, t0=t0: s.dma_start(out=gate_o[:, :, t0:t0 + 512], in_=gt[:]).then_inc(st, 16))
            gate_store = p.val(st)

        dlt_store = 0
        xp_last_mm = 0
        for it in range(NT):
            t0 = it * 512
            bankx, wx = rrxp.acquire()
            gvx = emit_mm_group(
                p, smm, bankx,
                [(lambda c=c: xpwb[:, c, :],
                  lambda c=c, t0=t0: u_sb[:, c, t0:t0 + 512]) for c in range(16)],
                extra_waits=wx + ([(sac, u_all_done)] if it == 0 else []))
            p.wait("scalar", smm, gvx)
            p.on("scalar", lambda s, bankx=bankx, t0=t0:
                 s.activation(out=xp_sb[:, t0:t0 + 512], in_=bankx[:], func=AF.Copy).then_inc(sac, 1))
            xp_done = p.bump(sac)
            rrxp.release(sac, xp_done)

            p.wait("vector", sac, xp_done)
            if it > 0:
                p.wait("vector", smm, xp_last_mm)
            p.on("vector", lambda v, t0=t0:
                 v.tensor_copy(xp64b[:], xp_sb[0:64, t0:t0 + 512]).then_inc(sdv, 1))
            xp64_done = p.bump(sdv)
            add_vals = {}
            for j in range(16):
                bank, w = rr.acquire()
                gv = emit_mm_group(
                    p, smm, bank,
                    [(lambda j=j: dtwb[:, j * 128:(j + 1) * 128], lambda: xp64b[:])],
                    extra_waits=w + ([(sdv, xp64_done)] if j == 0 else []))
                xp_last_mm = gv
                sl = j % 2
                p.wait("scalar", smm, gv)
                if j >= 2:
                    p.wait("scalar", sdv, add_vals[j - 2])
                p.on("scalar", lambda s, bank=bank, j=j, sl=sl:
                     s.activation(out=sp1[sl][:], in_=bank[:], func=AF.Abs, bias=dtbs[:, j:j + 1]))
                p.on("scalar", lambda s, sl=sl:
                     s.activation(out=sp2[sl][:], in_=sp1[sl][:], func=AF.Exp, scale=ccs[:, C_NEG1:C_NEG1 + 1]))
                p.on("scalar", lambda s, sl=sl:
                     s.activation(out=sp2[sl][:], in_=sp2[sl][:], func=AF.Ln, bias=ccs[:, C_ONE:C_ONE + 1]))
                p.on("scalar", lambda s, bank=bank, j=j, sl=sl:
                     s.activation(out=sp1[sl][:], in_=bank[:], func=AF.Relu,
                                  bias=dtbs[:, j:j + 1]).then_inc(sac, 1))
                spv = p.bump(sac)
                rr.release(sac, spv)
                p.wait("vector", sac, spv)
                if j == 0 and it > 0:
                    p.wait("vector", st, dlt_store)
                if j == 0 and it == 0:
                    p.wait("vector", st, gate_store)
                p.on("vector", lambda v, j=j, sl=sl:
                     v.tensor_add(dlt[:, j, :], sp1[sl][:], sp2[sl][:]).then_inc(sdv, 1))
                add_vals[j] = p.bump(sdv)
            p.wait("sync", sdv, p.val(sdv))
            p.bump(st, 16)
            p.on("sync", lambda s, t0=t0: s.dma_start(out=delta_o[:, :, t0:t0 + 512], in_=dlt[:]).then_inc(st, 16))
            dlt_store = p.val(st)

        p.bump(st, 16)
        p.on("sync", lambda s: s.dma_start(out=bc_o[:], in_=xp_sb[64:96, :]).then_inc(st, 16))
        p.wait("sync", st, p.val(st))

        with nc.Block() as block:
            p.finalize(block)
    return nc


def host_inputs_A(inp):
    cc = host_consts()
    W1r = fm(inp["L0_attn_W1"])
    W2r = fm(inp["L0_attn_W2"])
    XPWr = fm(inp["L0_attn_x_proj_w"])
    DTWr = np.ascontiguousarray(inp["L0_attn_dt_proj_w"])
    B1r = cols(inp["L0_attn_b1"])
    B2r = cols(inp["L0_attn_b2"])
    CWr = np.ascontiguousarray(inp["L0_attn_conv_w"].reshape(16, 128, 4).swapaxes(0, 1))
    CBr = cols(inp["L0_attn_conv_b"])
    DTBr = cols(inp["L0_attn_dt_proj_b"])
    x = inp["L0_attn_x"]
    maps = []
    for c in range(8):
        b, h = c // 2, c % 2
        xTf = np.ascontiguousarray(x[b].T)
        t0 = h * TH
        halo = np.zeros((DM, 3), np.float32) if h == 0 else xTf[:, t0 - 3:t0]
        xTh = fm(np.concatenate([halo, xTf[:, t0:t0 + TH]], axis=1))
        hs = np.full((128, 1), 0.0 if h == 0 else 1.0, np.float32)
        maps.append(dict(xT=xTh, W1=W1r, W2=W2r, XPW=XPWr, DTW=DTWr, B1=B1r,
                         B2=B2r, CW=CWr, CB=CBr, DTB=DTBr, HS=hs, CC=cc))
    return maps


def kernel(**inputs):
    raise NotImplementedError("under construction")



def build_B():
    nc = mknc()
    NCH = 8

    delta_i = nc.dram_tensor("delta_i", [128, 8, T], BF16, kind="ExternalInput").ap()
    u_i = nc.dram_tensor("u_i", [128, 8, T], BF16, kind="ExternalInput").ap()
    gate_i = nc.dram_tensor("gate_i", [128, 8, T], BF16, kind="ExternalInput").ap()
    bcB = nc.dram_tensor("bcB", [32, T], F32, kind="ExternalInput").ap()
    ALOG = nc.dram_tensor("ALOG", [128, 8, 16], F32, kind="ExternalInput").ap()
    DSK = nc.dram_tensor("DSK", [128, 8], F32, kind="ExternalInput").ap()
    SEL32 = nc.dram_tensor("SEL32", [32, 32 * 128], F32, kind="ExternalInput").ap()
    CC = nc.dram_tensor("CC", [128, NC], F32, kind="ExternalInput").ap()

    yg_o = nc.dram_tensor("yg_o", [128, 8, T], BF16, kind="ExternalOutput").ap()

    with ExitStack() as ctx:
        dbuf = [sbuf(nc, ctx, f"dbuf{i}", [128, 8, 512], BF16) for i in range(2)]
        ubuf = [sbuf(nc, ctx, f"ubuf{i}", [128, 8, 512], BF16) for i in range(2)]
        gbuf = [sbuf(nc, ctx, f"gbuf{i}", [128, 8, 512], BF16) for i in range(2)]
        Bbc = [sbuf(nc, ctx, f"Bbc{i}", [128, 16, 512], BF16) for i in range(2)]
        Cbc = [sbuf(nc, ctx, f"Cbc{i}", [128, 16, 512], BF16) for i in range(2)]
        bcbf = sbuf(nc, ctx, "bcbf", [32, T], BF16)
        sel32b = sbuf(nc, ctx, "sel32b", [32, 32 * 128], BF16)
        A_sb = sbuf(nc, ctx, "A_sb", [128, 8, 16], F32)
        dsk = sbuf(nc, ctx, "dsk", [128, 8], F32)
        ccs = sbuf(nc, ctx, "ccs", [128, NC], F32)
        state = sbuf(nc, ctx, "state", [128, 8, 16], F32)
        dAb = [sbuf(nc, ctx, f"dAb{i}", [128, 512], F32) for i in range(2)]
        dBu = [sbuf(nc, ctx, f"dBu{i}", [128, 512], BF16) for i in range(2)]
        dumm = sbuf(nc, ctx, "dumm", [128, 1], F32)
        htl = sbuf(nc, ctx, "htl", [128, 512], BF16)
        hcb = sbuf(nc, ctx, "hcb", [128, 512], BF16)
        dut = sbuf(nc, ctx, "dut", [128, 8, 512], BF16)
        y_sb = sbuf(nc, ctx, "y_sb", [128, 8, 512], BF16)
        ot = [sbuf(nc, ctx, f"ot{i}", [128, 8, 512], BF16) for i in range(2)]

        ps_bc = [psum(nc, ctx, f"ps_bc{i}", [128, 512], F32) for i in range(2)]

        p = Prog(nc, ctx)
        ldp = [p.sem("ld0"), p.sem("ld1")]
        lg = p.sem("lg")
        smm = p.sem("smm")
        sac = p.sem("sac")
        sdv = p.sem("sdv")
        stp = [p.sem("st0"), p.sem("st1")]
        rrbc = RR(p, ps_bc)

        for sb_, dr in [(bcbf, bcB), (sel32b, SEL32)]:
            p.bump(lg, 16)
            p.on("gpsimd", lambda g, sb_=sb_, dr=dr: g.dma_start(out=sb_[:], in_=dr[:]).then_inc(lg, 16))
        for sb_, dr in [(A_sb, ALOG), (dsk, DSK), (ccs, CC)]:
            p.bump(lg, 16)
            p.on("gpsimd", lambda g, sb_=sb_, dr=dr: g.dma_start(out=sb_[:], in_=dr[:]).then_inc(lg, 16))
        lg_all = p.val(lg)

        p.wait("scalar", lg, lg_all)
        p.on("scalar", lambda s: s.activation(out=A_sb[:], in_=A_sb[:], func=AF.Exp).then_inc(sac, 1))
        a_exp = p.bump(sac)
        p.wait("vector", sac, a_exp)
        p.on("vector", lambda v: v.tensor_scalar_mul(A_sb[:], A_sb[:], -1.0))
        p.on("vector", lambda v: v.memset(state[:], 0.0).then_inc(sdv, 1))
        init_done = p.bump(sdv)

        def emit_chunk_loads(k, waitv):
            t0 = k * 512
            sem = ldp[k % 2]
            if waitv:
                p.wait("sync", sdv, waitv)
            for buf, src in [(dbuf[k % 2], delta_i), (ubuf[k % 2], u_i), (gbuf[k % 2], gate_i)]:
                p.bump(sem, 16)
                p.on("sync", lambda s, buf=buf, src=src, t0=t0, sem=sem:
                     s.dma_start(out=buf[:], in_=src[:, :, t0:t0 + 512]).then_inc(sem, 16))
            return (sem, p.val(sem))

        ld_done = [None] * NCH
        scan_done = [None] * NCH
        store_done = [None] * NCH
        bc_done = [None] * NCH
        ld_done[0] = emit_chunk_loads(0, 0)
        ld_done[1] = emit_chunk_loads(1, 0)

        tts_done = {}
        for k in range(NCH):
            t0 = k * 512
            for n in range(32):
                bank, w = rrbc.acquire()
                gv = emit_mm_group(
                    p, smm, bank,
                    [(lambda n=n: sel32b[:, n * 128:(n + 1) * 128],
                      lambda t0=t0: bcbf[:, t0:t0 + 512])],
                    extra_waits=w + ([(lg, lg_all)] if (k == 0 and n == 0) else []))
                p.wait("scalar", smm, gv)
                if k >= 2 and n == 0:
                    p.wait("scalar", sdv, scan_done[k - 2])
                dst = Bbc[k % 2] if n < 16 else Cbc[k % 2]
                p.on("scalar", lambda s, bank=bank, dst=dst, n=n:
                     s.activation(out=dst[:, n % 16, :], in_=bank[:], func=AF.Copy).then_inc(sac, 1))
                rrbc.release(sac, p.bump(sac))
            bc_done[k] = p.val(sac)

            p.wait("vector", ld_done[k][0], ld_done[k][1])
            if k == 0:
                p.wait("vector", sdv, init_done)
            p.on("vector", lambda v, k=k:
                 v.tensor_mul(dut[:], dbuf[k % 2][:], ubuf[k % 2][:]).then_inc(sdv, 1))
            p.bump(sdv)

            for c in range(8):
                for n in range(16):
                    idx = k * 128 + c * 16 + n
                    sl = idx % 2
                    if idx >= 2:
                        p.wait("scalar", sdv, tts_done[idx - 2])
                    if c == 0 and n == 0:
                        p.wait("scalar", ld_done[k][0], ld_done[k][1])
                    if k == 0 and c == 0 and n == 0:
                        p.wait("scalar", sdv, init_done)
                    p.on("scalar", lambda s, k=k, c=c, n=n, sl=sl:
                         s.activation(out=dAb[sl][:], in_=dbuf[k % 2][:, c, :], func=AF.Exp,
                                      scale=A_sb[:, c, n:n + 1]).then_inc(sac, 1))
                    dav = p.bump(sac)
                    p.wait("vector", sac, dav)
                    if c == 0 and n == 0:
                        p.wait("vector", sac, bc_done[k])
                        p.on("vector", lambda v, k=k, c=c, n=n:
                             v.tensor_mul(dBu[n % 2][:], dut[:, c, :], Bbc[k % 2][:, n, :]))
                    p.on("vector", lambda v, sl=sl, c=c, n=n:
                         v.tensor_tensor_scan(out=htl[:], data0=dAb[sl][:], data1=dBu[n % 2][:],
                                              initial=state[:, c, n:n + 1],
                                              op0=ALU.mult, op1=ALU.add))
                    if n < 15:
                        p.on("vector", lambda v, k=k, c=c, n=n:
                             v.tensor_mul(dBu[(n + 1) % 2][:], dut[:, c, :], Bbc[k % 2][:, n + 1, :]))
                    elif c < 7:
                        p.on("vector", lambda v, k=k, c=c:
                             v.tensor_mul(dBu[0][:], dut[:, c + 1, :], Bbc[k % 2][:, 0, :]))
                    else:
                        p.on("vector", lambda v: v.tensor_copy(dumm[:], state[:, 0, 0:1]))
                    p.on("vector", lambda v, c=c, n=n:
                         v.tensor_copy(state[:, c, n:n + 1], htl[:, 511:512]))
                    if n == 0:
                        p.on("vector", lambda v, k=k, c=c, n=n:
                             v.tensor_mul(y_sb[:, c, :], htl[:], Cbc[k % 2][:, n, :]).then_inc(sdv, 1))
                    else:
                        p.on("vector", lambda v, k=k, c=c, n=n:
                             v.tensor_mul(hcb[:], htl[:], Cbc[k % 2][:, n, :]))
                        p.on("vector", lambda v, c=c:
                             v.tensor_add(y_sb[:, c, :], y_sb[:, c, :], hcb[:]).then_inc(sdv, 1))
                    tts_done[idx] = p.bump(sdv)
                if k >= 2 and c == 0:
                    p.wait("vector", stp[k % 2], store_done[k - 2])
                p.on("vector", lambda v, k=k, c=c:
                     v.scalar_tensor_tensor(out=y_sb[:, c, :], in0=ubuf[k % 2][:, c, :],
                                            scalar=dsk[:, c:c + 1], in1=y_sb[:, c, :],
                                            op0=ALU.mult, op1=ALU.add))
                p.on("vector", lambda v, k=k, c=c:
                     v.tensor_mul(ot[k % 2][:, c, :], y_sb[:, c, :], gbuf[k % 2][:, c, :]).then_inc(sdv, 1))
                p.bump(sdv)
            scan_done[k] = p.val(sdv)

            p.wait("sync", sdv, scan_done[k])
            p.bump(stp[k % 2], 16)
            p.on("sync", lambda s, k=k, t0=t0:
                 s.dma_start(out=yg_o[:, :, t0:t0 + 512], in_=ot[k % 2][:]).then_inc(stp[k % 2], 16))
            store_done[k] = p.val(stp[k % 2])
            if k + 2 < NCH:
                ld_done[k + 2] = emit_chunk_loads(k + 2, scan_done[k])

        p.wait("sync", stp[0], p.val(stp[0]))
        p.wait("sync", stp[1], p.val(stp[1]))
        with nc.Block() as block:
            p.finalize(block)
    return nc


def host_inputs_B(a_outs, inp):
    cc = host_consts()
    sel = np.zeros((32, 32 * 128), np.float32)
    for n in range(32):
        sel[n, n * 128:(n + 1) * 128] = 1.0
    A_log = inp["L0_attn_A_log"]
    Dsk = inp["L0_attn_D_skip"]
    maps = []
    for c in range(8):
        b, dh = c // 2, c % 2
        e0, e1 = a_outs[2 * b], a_outs[2 * b + 1]
        csl = slice(dh * 8, dh * 8 + 8)
        delta = np.concatenate([e0["delta_o"], e1["delta_o"]], axis=2)[:, csl]
        u = np.concatenate([e0["u_o"], e1["u_o"]], axis=2)[:, csl]
        gate = np.concatenate([e0["gate_o"], e1["gate_o"]], axis=2)[:, csl]
        bc = np.concatenate([e0["bc_o"], e1["bc_o"]], axis=1)
        alog = np.ascontiguousarray(
            A_log[dh * 1024:(dh + 1) * 1024].reshape(8, 128, 16).swapaxes(0, 1))
        dskr = cols(Dsk[dh * 1024:(dh + 1) * 1024])
        maps.append(dict(delta_i=np.ascontiguousarray(delta), u_i=np.ascontiguousarray(u),
                         gate_i=np.ascontiguousarray(gate), bcB=bc, ALOG=alog,
                         DSK=dskr, SEL32=sel, CC=cc))
    return maps



def emit_ln(p, env, src_view, outs, W, waits=()):
    s1b, sqb = env["s1b"], env["sqb"]
    m_sb, r_sb, t1 = env["m_sb"], env["r_sb"], env["t1"]
    df = env["df"]
    ones128b, ones1f = env["ones128b"], env["ones1f"]
    ccs = env["ccs"]
    rrst, rrbc2 = env["rrst"], env["rrbc2"]
    smm, sac, sdv = env["smm"], env["sac"], env["sdv"]
    out_f32, out_b16 = outs

    for hh, vv in waits:
        p.wait("vector", hh, vv)
    p.on("vector", lambda v: v.tensor_copy(s1b[:, :, 0:W], src_view()).then_inc(sdv, 1))
    cast_done = p.bump(sdv)
    p.wait("scalar", sdv, cast_done)
    p.on("scalar", lambda s: s.activation(out=sqb[:, :, 0:W], in_=s1b[:, :, 0:W],
                                          func=AF.Square).then_inc(sac, 1))
    sq_done = p.bump(sac)

    bankA, wA = rrst.acquire()
    gvA = emit_mm_group(p, smm, bankA,
                        [(lambda: ones128b[:], lambda c=c: s1b[:, c, 0:W]) for c in range(8)],
                        extra_waits=wA + [(sdv, cast_done)], width=W, rows=1)
    bankB, wB = rrst.acquire()
    gvB = emit_mm_group(p, smm, bankB,
                        [(lambda: ones128b[:], lambda c=c: sqb[:, c, 0:W]) for c in range(8)],
                        extra_waits=wB + [(sac, sq_done)], width=W, rows=1)
    p.wait("vector", smm, gvB)
    p.on("vector", lambda v: v.tensor_scalar_mul(m_sb[:, 0:W], bankA[0:1, 0:W], 1.0 / 1024.0))
    p.on("vector", lambda v: v.tensor_scalar_mul(r_sb[:, 0:W], bankB[0:1, 0:W], 1.0 / 1024.0))
    p.on("vector", lambda v: v.tensor_mul(t1[:, 0:W], m_sb[:, 0:W], m_sb[:, 0:W]))
    p.on("vector", lambda v: v.tensor_sub(r_sb[:, 0:W], r_sb[:, 0:W], t1[:, 0:W]).then_inc(sdv, 1))
    var_done = p.bump(sdv)
    rrst.release(sdv, var_done)
    rrst.release(sdv, var_done)
    p.wait("scalar", sdv, var_done)
    p.on("scalar", lambda s: s.activation(out=r_sb[:, 0:W], in_=r_sb[:, 0:W], func=AF.Sqrt,
                                          bias=ccs[0:1, C_EPS:C_EPS + 1]).then_inc(sac, 1))
    std_done = p.bump(sac)
    p.wait("vector", sac, std_done)
    p.on("vector", lambda v: v.reciprocal(r_sb[:, 0:W], r_sb[:, 0:W]).then_inc(sdv, 1))
    rstd_done = p.bump(sdv)

    bankM, wM = rrbc2.acquire()
    emit_mm_group(p, smm, bankM, [(lambda: ones1f[:], lambda: m_sb[:, 0:W])],
                  extra_waits=wM + [(sdv, var_done)], width=W)
    bankR, wR = rrbc2.acquire()
    gvR = emit_mm_group(p, smm, bankR, [(lambda: ones1f[:], lambda: r_sb[:, 0:W])],
                        extra_waits=wR + [(sdv, rstd_done)], width=W)
    p.wait("vector", smm, gvR)
    for c in range(8):
        p.on("vector", lambda v, c=c:
             v.tensor_sub(df[:, c, 0:W], src_view()[:, c, :], bankM[:, 0:W]))
    for c in range(8):
        is_last = c == 7
        if out_f32 is not None:
            if out_b16 is None and is_last:
                p.on("vector", lambda v, c=c:
                     v.tensor_mul(out_f32()[:, c, :], bankR[:, 0:W], df[:, c, 0:W]).then_inc(sdv, 1))
            else:
                p.on("vector", lambda v, c=c:
                     v.tensor_mul(out_f32()[:, c, :], bankR[:, 0:W], df[:, c, 0:W]))
            if out_b16 is not None:
                def mk(c, fin):
                    if fin:
                        return lambda v: v.tensor_copy(out_b16()[:, c, :], out_f32()[:, c, :]).then_inc(sdv, 1)
                    return lambda v: v.tensor_copy(out_b16()[:, c, :], out_f32()[:, c, :])
                p.on("vector", mk(c, is_last))
        else:
            if is_last:
                p.on("vector", lambda v, c=c:
                     v.tensor_mul(out_b16()[:, c, :], bankR[:, 0:W], df[:, c, 0:W]).then_inc(sdv, 1))
            else:
                p.on("vector", lambda v, c=c:
                     v.tensor_mul(out_b16()[:, c, :], bankR[:, 0:W], df[:, c, 0:W]))
    done = p.bump(sdv)
    rrbc2.release(sdv, done)
    rrbc2.release(sdv, done)
    return done


def build_C():
    nc = mknc()
    TT = TH
    NT = TT // 512

    YG = nc.dram_tensor("YG", [128, 16, TT], BF16, kind="ExternalInput").ap()
    WO = nc.dram_tensor("WO", [128, 16, 1024], F32, kind="ExternalInput").ap()
    BO = nc.dram_tensor("BO", [128, 8], F32, kind="ExternalInput").ap()
    XT = nc.dram_tensor("XT", [128, 8, TT], F32, kind="ExternalInput").ap()
    ONES128 = nc.dram_tensor("ONES128", [128, 1], F32, kind="ExternalInput").ap()
    ONES1 = nc.dram_tensor("ONES1", [1, 128], F32, kind="ExternalInput").ap()
    CC = nc.dram_tensor("CC", [128, NC], F32, kind="ExternalInput").ap()

    s2f_o = nc.dram_tensor("s2f_o", [128, 8, TT], F32, kind="ExternalOutput").ap()
    s2b_o = nc.dram_tensor("s2b_o", [128, 8, TT], BF16, kind="ExternalOutput").ap()

    with ExitStack() as ctx:
        wob = sbuf(nc, ctx, "wob", [128, 16, 1024], BF16)
        bos = sbuf(nc, ctx, "bos", [128, 8], F32)
        ccs = sbuf(nc, ctx, "ccs", [128, NC], F32)
        ones128b = sbuf(nc, ctx, "ones128b", [128, 1], BF16)
        ones1f = sbuf(nc, ctx, "ones1f", [1, 128], F32)
        ygt = [sbuf(nc, ctx, f"ygt{i}", [128, 16, 512], BF16) for i in range(2)]
        xtt = [sbuf(nc, ctx, f"xtt{i}", [128, 8, 512], F32) for i in range(2)]
        ao = sbuf(nc, ctx, "ao", [128, 8, 512], F32)
        s1b = sbuf(nc, ctx, "s1b", [128, 8, 512], BF16)
        sqb = sbuf(nc, ctx, "sqb", [128, 8, 512], BF16)
        df = sbuf(nc, ctx, "df", [128, 8, 512], F32)
        m_sb = sbuf(nc, ctx, "m_sb", [1, 512], F32)
        r_sb = sbuf(nc, ctx, "r_sb", [1, 512], F32)
        t1 = sbuf(nc, ctx, "t1", [1, 512], F32)
        of = [sbuf(nc, ctx, f"of{i}", [128, 8, 512], F32) for i in range(2)]
        ob = [sbuf(nc, ctx, f"ob{i}", [128, 8, 512], BF16) for i in range(2)]

        banks = [psum(nc, ctx, f"bank{i}", [128, 512], F32) for i in range(2)]
        stbanks = [psum(nc, ctx, f"stb{i}", [128, 512], F32) for i in range(2)]
        bcbanks = [psum(nc, ctx, f"bcb{i}", [128, 512], F32) for i in range(2)]

        p = Prog(nc, ctx)
        lg = p.sem("lg")
        ldp = [p.sem("ld0"), p.sem("ld1")]
        smm = p.sem("smm")
        sac = p.sem("sac")
        sdv = p.sem("sdv")
        stp = [p.sem("st0"), p.sem("st1")]
        rr = RR(p, banks)
        rrst = RR(p, stbanks)
        rrbc2 = RR(p, bcbanks)
        env = dict(s1b=s1b, sqb=sqb, m_sb=m_sb, r_sb=r_sb, t1=t1, df=df,
                   ones128b=ones128b, ones1f=ones1f, ccs=ccs,
                   rrst=rrst, rrbc2=rrbc2, smm=smm, sac=sac, sdv=sdv)

        p.bump(lg, 16)
        p.on("gpsimd", lambda g: g.dma_start(out=wob[:], in_=WO[:]).then_inc(lg, 16))
        p.bump(lg, 16)
        p.on("gpsimd", lambda g: g.dma_start(out=ones128b[:], in_=ONES128[:]).then_inc(lg, 16))
        for sb_, dr in [(bos, BO), (ccs, CC), (ones1f, ONES1)]:
            p.bump(lg, 16)
            p.on("gpsimd", lambda g, sb_=sb_, dr=dr: g.dma_start(out=sb_[:], in_=dr[:]).then_inc(lg, 16))
        lg_all = p.val(lg)

        ln_done = [0, 0]
        ln_last = 0
        store_done = [0, 0]
        ld_done = [None] * NT
        for it in range(min(2, NT)):
            t0 = it * 512
            for buf, src in [(ygt[it % 2], YG), (xtt[it % 2], XT)]:
                p.bump(ldp[it % 2], 16)
                p.on("sync", lambda s, buf=buf, src=src, t0=t0, it=it:
                     s.dma_start(out=buf[:], in_=src[:, :, t0:t0 + 512]).then_inc(ldp[it % 2], 16))
            ld_done[it] = p.val(ldp[it % 2])

        for it in range(NT):
            t0 = it * 512
            par = it % 2
            ev_done = 0
            for j in range(8):
                bank, w = rr.acquire()
                gv = emit_mm_group(
                    p, smm, bank,
                    [(lambda c=c, j=j: wob[:, c, j * 128:(j + 1) * 128],
                      lambda c=c, par=par: ygt[par][:, c, :]) for c in range(16)],
                    extra_waits=w + ([(lg, lg_all), (ldp[par], ld_done[it])] if j == 0 else []))
                p.wait("scalar", smm, gv)
                if j == 0 and it >= 1:
                    p.wait("scalar", sdv, ln_last)
                p.on("scalar", lambda s, bank=bank, j=j:
                     s.activation(out=ao[:, j, :], in_=bank[:], func=AF.Identity,
                                  bias=bos[:, j:j + 1]).then_inc(sac, 1))
                ev_done = p.bump(sac)
                rr.release(sac, ev_done)
            p.wait("vector", sac, ev_done)
            p.wait("vector", ldp[par], ld_done[it])
            for c in range(8):
                if c == 7:
                    p.on("vector", lambda v, c=c, par=par:
                         v.tensor_add(ao[:, c, :], ao[:, c, :], xtt[par][:, c, :]).then_inc(sdv, 1))
                else:
                    p.on("vector", lambda v, c=c, par=par:
                         v.tensor_add(ao[:, c, :], ao[:, c, :], xtt[par][:, c, :]))
            res_done = p.bump(sdv)
            lnw = [(sdv, res_done)]
            if it >= 2:
                lnw.append((stp[par], store_done[par]))
            ln_done[par] = ln_last = emit_ln(
                p, env, lambda: ao[:],
                (lambda par=par: of[par][:], lambda par=par: ob[par][:]),
                512, waits=lnw)
            p.wait("sync", sdv, ln_done[par])
            p.bump(stp[par], 32)
            p.on("sync", lambda s, t0=t0, par=par:
                 s.dma_start(out=s2f_o[:, :, t0:t0 + 512], in_=of[par][:]).then_inc(stp[par], 16))
            p.on("sync", lambda s, t0=t0, par=par:
                 s.dma_start(out=s2b_o[:, :, t0:t0 + 512], in_=ob[par][:]).then_inc(stp[par], 16))
            store_done[par] = p.val(stp[par])
            if it + 2 < NT:
                nxt = it + 2
                nt0 = nxt * 512
                p.wait("sync", sdv, ln_done[par])
                for buf, src in [(ygt[par], YG), (xtt[par], XT)]:
                    p.bump(ldp[par], 16)
                    p.on("sync", lambda s, buf=buf, src=src, nt0=nt0, par=par:
                         s.dma_start(out=buf[:], in_=src[:, :, nt0:nt0 + 512]).then_inc(ldp[par], 16))
                ld_done[nxt] = p.val(ldp[par])

        p.wait("sync", stp[0], p.val(stp[0]))
        p.wait("sync", stp[1], p.val(stp[1]))
        with nc.Block() as block:
            p.finalize(block)
    return nc


def host_inputs_C(b_outs, inp):
    cc = host_consts()
    ones128 = np.ones((128, 1), np.float32)
    ones1 = np.ones((1, 128), np.float32)
    WOr = fm(inp["L0_attn_Wo"])
    BOr = cols(inp["L0_attn_bo"])
    x = inp["L0_attn_x"]
    maps = []
    for c in range(8):
        b, h = c // 2, c % 2
        t0 = h * TH
        yg = np.concatenate([b_outs[2 * b]["yg_o"], b_outs[2 * b + 1]["yg_o"]], axis=1)
        yg = np.ascontiguousarray(yg[:, :, t0:t0 + TH])
        xT = fm(np.ascontiguousarray(x[b].T[:, t0:t0 + TH]))
        maps.append(dict(YG=yg, WO=WOr, BO=BOr, XT=xT, ONES128=ones128, ONES1=ones1, CC=cc))
    return maps


def build_D():
    nc = mknc()
    TT = TH
    NT = TT // 256

    W1N = nc.dram_tensor("W1N", [128, 32, 1024], F32, kind="ExternalInput").ap()
    W1T = nc.dram_tensor("W1T", [128, 8, 4096], F32, kind="ExternalInput").ap()
    W2N = nc.dram_tensor("W2N", [128, 8, 4096], F32, kind="ExternalInput").ap()
    W2T = nc.dram_tensor("W2T", [128, 32, 1024], F32, kind="ExternalInput").ap()
    B1C = nc.dram_tensor("B1C", [128, 32], F32, kind="ExternalInput").ap()
    B2C = nc.dram_tensor("B2C", [128, 8], F32, kind="ExternalInput").ap()
    S2F = nc.dram_tensor("S2F", [128, 8, TT], F32, kind="ExternalInput").ap()
    S2B = nc.dram_tensor("S2B", [128, 8, TT], BF16, kind="ExternalInput").ap()
    ONES128 = nc.dram_tensor("ONES128", [128, 1], F32, kind="ExternalInput").ap()
    ONES1 = nc.dram_tensor("ONES1", [1, 128], F32, kind="ExternalInput").ap()
    CC = nc.dram_tensor("CC", [128, NC], F32, kind="ExternalInput").ap()

    s3f_o = nc.dram_tensor("s3f_o", [128, 8, TT], F32, kind="ExternalOutput").ap()
    s3b_o = nc.dram_tensor("s3b_o", [128, 8, TT], BF16, kind="ExternalOutput").ap()
    dbg_wu = nc.dram_tensor("dbg_wu", [128, 8, 4096], BF16, kind="ExternalOutput").ap()
    dbg_wd = nc.dram_tensor("dbg_wd", [128, 32, 1024], BF16, kind="ExternalOutput").ap()
    dbg_al = nc.dram_tensor("dbg_al", [128, 40], F32, kind="ExternalOutput").ap()
    dbg_hh = nc.dram_tensor("dbg_hh", [128, 32, 256], BF16, kind="ExternalOutput").ap()
    dbg_ft = nc.dram_tensor("dbg_ft", [128, 8, 256], F32, kind="ExternalOutput").ap()

    with ExitStack() as ctx:
        w_up = sbuf(nc, ctx, "w_up", [128, 8, 4096], BF16)
        w_down = sbuf(nc, ctx, "w_down", [128, 32, 1024], BF16)
        al1 = sbuf(nc, ctx, "al1", [128, 32], F32)
        al2 = sbuf(nc, ctx, "al2", [128, 8], F32)
        b1c = sbuf(nc, ctx, "b1c", [128, 32], F32)
        b2c = sbuf(nc, ctx, "b2c", [128, 8], F32)
        ccs = sbuf(nc, ctx, "ccs", [128, NC], F32)
        ones128b = sbuf(nc, ctx, "ones128b", [128, 1], BF16)
        ones1f = sbuf(nc, ctx, "ones1f", [1, 128], F32)
        s2bt = [sbuf(nc, ctx, f"s2bt{i}", [128, 8, 256], BF16) for i in range(2)]
        s2ft = [sbuf(nc, ctx, f"s2ft{i}", [128, 8, 256], F32) for i in range(2)]
        hh = sbuf(nc, ctx, "hh", [128, 32, 256], BF16)
        ft = sbuf(nc, ctx, "ft", [128, 8, 256], F32)
        s1b = sbuf(nc, ctx, "s1b", [128, 8, 256], BF16)
        sqb = sbuf(nc, ctx, "sqb", [128, 8, 256], BF16)
        df = ft
        m_sb = sbuf(nc, ctx, "m_sb", [1, 256], F32)
        r_sb = sbuf(nc, ctx, "r_sb", [1, 256], F32)
        t1 = sbuf(nc, ctx, "t1", [1, 256], F32)
        of = [sbuf(nc, ctx, f"of{i}", [128, 8, 256], F32) for i in range(2)]

        banks = [psum(nc, ctx, f"bank{i}", [128, 512], F32) for i in range(2)]
        stbanks = [psum(nc, ctx, f"stb{i}", [128, 512], F32) for i in range(2)]
        bcbanks = [psum(nc, ctx, f"bcb{i}", [128, 512], F32) for i in range(2)]

        p = Prog(nc, ctx)
        lg = p.sem("lg")
        lw = p.sem("lw")
        ldp = [p.sem("ld0"), p.sem("ld1")]
        smm = p.sem("smm")
        sac = p.sem("sac")
        sdv = p.sem("sdv")
        stp = [p.sem("st0"), p.sem("st1")]
        stg = [p.sem("sg0"), p.sem("sg1")]
        rr = RR(p, banks)
        rrst = RR(p, stbanks)
        rrbc2 = RR(p, bcbanks)
        env = dict(s1b=s1b, sqb=sqb, m_sb=m_sb, r_sb=r_sb, t1=t1, df=df,
                   ones128b=ones128b, ones1f=ones1f, ccs=ccs,
                   rrst=rrst, rrbc2=rrbc2, smm=smm, sac=sac, sdv=sdv)

        for sb_, dr in [(b1c, B1C), (b2c, B2C), (ccs, CC), (ones1f, ONES1)]:
            p.bump(lg, 16)
            p.on("gpsimd", lambda g, sb_=sb_, dr=dr: g.dma_start(out=sb_[:], in_=dr[:]).then_inc(lg, 16))
        p.bump(lg, 16)
        p.on("gpsimd", lambda g: g.dma_start(out=ones128b[:], in_=ONES128[:]).then_inc(lg, 16))
        lg_small = p.val(lg)

        p.bump(lw, 16)
        p.on("gpsimd", lambda g: g.dma_start(out=w_down[:], in_=W1N[:]).then_inc(lw, 16))
        p.bump(lw, 16)
        p.on("gpsimd", lambda g: g.dma_start(out=w_up[:], in_=W2N[:]).then_inc(lw, 16))
        lw_a = p.val(lw)
        p.wait("vector", lw, lw_a)
        for c in range(32):
            p.on("vector", lambda v, c=c: v.reduce_sum(al1[:, c:c + 1], w_down[:, c, :],
                                                       axis=mybir.AxisListType.X,
                                                       apply_absolute_value=True))
        for c in range(8):
            p.on("vector", lambda v, c=c: v.reduce_sum(al2[:, c:c + 1], w_up[:, c, :],
                                                       axis=mybir.AxisListType.X,
                                                       apply_absolute_value=True))
        p.on("vector", lambda v: v.tensor_scalar_mul(al1[:], al1[:], 1.0 / 1024.0))
        p.on("vector", lambda v: v.tensor_scalar_mul(al2[:], al2[:], 1.0 / 4096.0).then_inc(sdv, 1))
        alpha_done = p.bump(sdv)
        p.wait("gpsimd", sdv, alpha_done)
        p.bump(lw, 16)
        p.on("gpsimd", lambda g: g.dma_start(out=w_up[:], in_=W1T[:]).then_inc(lw, 16))
        p.bump(lw, 16)
        p.on("gpsimd", lambda g: g.dma_start(out=w_down[:], in_=W2T[:]).then_inc(lw, 16))
        lw_b = p.val(lw)
        p.wait("scalar", lw, lw_b)
        for c in range(8):
            p.on("scalar", lambda s, c=c: s.activation(out=w_up[:, c, :], in_=w_up[:, c, :], func=AF.Sign))
        for c in range(32):
            last = c == 31
            if last:
                p.on("scalar", lambda s, c=c: s.activation(out=w_down[:, c, :], in_=w_down[:, c, :],
                                                           func=AF.Sign).then_inc(sac, 1))
            else:
                p.on("scalar", lambda s, c=c: s.activation(out=w_down[:, c, :], in_=w_down[:, c, :], func=AF.Sign))
        sign_done = p.bump(sac)
        p.wait("sync", sac, sign_done)
        p.on("sync", lambda s: s.dma_start(out=dbg_wu[:], in_=w_up[:]).then_inc(stp[0], 16))
        p.on("sync", lambda s: s.dma_start(out=dbg_wd[:], in_=w_down[:]).then_inc(stp[0], 16))
        p.on("sync", lambda s: s.dma_start(out=dbg_al[:, 0:32], in_=al1[:]).then_inc(stp[0], 16))
        p.on("sync", lambda s: s.dma_start(out=dbg_al[:, 32:40], in_=al2[:]).then_inc(stp[0], 16))
        p.bump(stp[0], 64)

        ld_done = [None] * NT
        ln_done = [0, 0]
        ln_last = 0
        store_done = [(0, 0), (0, 0)]

        def tile_loads(it, waits):
            par = it % 2
            t0 = it * 256
            for hh_, vv in waits:
                p.wait("sync", hh_, vv)
            for buf, src in [(s2bt[par], S2B), (s2ft[par], S2F)]:
                p.bump(ldp[par], 16)
                p.on("sync", lambda s, buf=buf, src=src, t0=t0, par=par:
                     s.dma_start(out=buf[:], in_=src[:, :, t0:t0 + 256]).then_inc(ldp[par], 16))
            ld_done[it] = p.val(ldp[par])

        tile_loads(0, [])
        tile_loads(1, [])

        dn_last = 0
        for it in range(NT):
            par = it % 2
            t0 = it * 256
            up_ev = 0
            for hb in range(32):
                bank, w = rr.acquire()
                ew = list(w)
                if hb == 0:
                    ew += [(lw, lw_b), (sac, sign_done), (ldp[par], ld_done[it])]
                gv = emit_mm_group(
                    p, smm, bank,
                    [(lambda c=c, hb=hb: w_up[:, c, hb * 128:(hb + 1) * 128],
                      lambda c=c, par=par: s2bt[par][:, c, :]) for c in range(8)],
                    extra_waits=ew, width=256)
                p.wait("scalar", smm, gv)
                if hb == 0 and it > 0:
                    p.wait("scalar", smm, dn_last)
                if hb == 0 and it == 1:
                    p.wait("scalar", stp[0], dbg_hh_done)
                if hb == 0:
                    p.wait("scalar", lg, lg_small)
                if hb == 0 and it == 0:
                    p.wait("scalar", sdv, alpha_done)
                p.on("scalar", lambda s, bank=bank, hb=hb:
                     s.activation(out=hh[:, hb, :], in_=bank[:, 0:256], func=AF.Gelu,
                                  scale=al1[:, hb:hb + 1], bias=b1c[:, hb:hb + 1]).then_inc(sac, 1))
                up_ev = p.bump(sac)
                rr.release(sac, up_ev)
            if it == 0:
                p.wait("sync", sac, up_ev)
                p.on("sync", lambda s: s.dma_start(out=dbg_hh[:], in_=hh[:]).then_inc(stp[0], 16))
                dbg_hh_done = p.bump(stp[0], 16)
            for j in range(8):
                bank, w = rr.acquire()
                ew = list(w)
                if j == 0:
                    ew += [(sac, up_ev)]
                gv = emit_mm_group(
                    p, smm, bank,
                    [(lambda hc=hc, j=j: w_down[:, hc, j * 128:(j + 1) * 128],
                      lambda hc=hc: hh[:, hc, :]) for hc in range(32)],
                    extra_waits=ew, width=256)
                dn_last = gv
                p.wait("scalar", smm, gv)
                if j == 0 and it >= 1:
                    p.wait("scalar", sdv, ln_last)
                if j == 0 and it == 1:
                    p.wait("scalar", stp[0], dbg_ft_done)
                p.on("scalar", lambda s, bank=bank, j=j:
                     s.activation(out=ft[:, j, :], in_=bank[:, 0:256], func=AF.Identity,
                                  scale=al2[:, j:j + 1], bias=b2c[:, j:j + 1]).then_inc(sac, 1))
                rr.release(sac, p.bump(sac))
            ev_done = p.val(sac)
            p.wait("vector", sac, ev_done)
            p.wait("vector", ldp[par], ld_done[it])
            for c in range(8):
                if c == 7:
                    p.on("vector", lambda v, c=c, par=par:
                         v.tensor_add(ft[:, c, :], ft[:, c, :], s2ft[par][:, c, :]).then_inc(sdv, 1))
                else:
                    p.on("vector", lambda v, c=c, par=par:
                         v.tensor_add(ft[:, c, :], ft[:, c, :], s2ft[par][:, c, :]))
            res_done = p.bump(sdv)
            if it == 0:
                p.wait("sync", sdv, res_done)
                p.on("sync", lambda s: s.dma_start(out=dbg_ft[:], in_=ft[:]).then_inc(stp[0], 16))
                dbg_ft_done = p.bump(stp[0], 16)
            lnw = [(sdv, res_done)]
            if it >= 2:
                lnw.append((stp[par], store_done[par][0]))
                lnw.append((stg[par], store_done[par][1]))
            ln_done[par] = ln_last = emit_ln(
                p, env, lambda: ft[:],
                (lambda par=par: of[par][:], None),
                256, waits=lnw)
            p.wait("sync", sdv, ln_done[par])
            p.wait("gpsimd", sdv, ln_done[par])
            p.bump(stp[par], 16)
            p.bump(stg[par], 16)
            p.on("sync", lambda s, t0=t0, par=par:
                 s.dma_start(out=s3f_o[:, :, t0:t0 + 256], in_=of[par][:]).then_inc(stp[par], 16))
            p.on("gpsimd", lambda g, t0=t0, par=par:
                 g.dma_start(out=s3b_o[:, :, t0:t0 + 256], in_=of[par][:]).then_inc(stg[par], 16))
            store_done[par] = (p.val(stp[par]), p.val(stg[par]))
            if it + 2 < NT:
                tile_loads(it + 2, [(sdv, ln_done[par])])

        p.wait("sync", stp[0], p.val(stp[0]))
        p.wait("sync", stp[1], p.val(stp[1]))
        p.wait("sync", stg[0], p.val(stg[0]))
        p.wait("sync", stg[1], p.val(stg[1]))
        with nc.Block() as block:
            p.finalize(block)
    return nc


def host_inputs_D(c_outs, inp):
    cc = host_consts()
    ones128 = np.ones((128, 1), np.float32)
    ones1 = np.ones((1, 128), np.float32)
    w1, w2 = inp["L0_ffn_w1"], inp["L0_ffn_w2"]
    W1Nr = fm(w1)
    W1Tr = fm(np.ascontiguousarray(w1.T))
    W2Nr = fm(w2)
    W2Tr = fm(np.ascontiguousarray(w2.T))
    B1Cr = cols(inp["L0_ffn_b1"])
    B2Cr = cols(inp["L0_ffn_b2"])
    maps = []
    for c in range(8):
        maps.append(dict(W1N=W1Nr, W1T=W1Tr, W2N=W2Nr, W2T=W2Tr, B1C=B1Cr, B2C=B2Cr,
                         S2F=c_outs[c]["s2f_o"], S2B=c_outs[c]["s2b_o"],
                         ONES128=ones128, ONES1=ones1, CC=cc))
    return maps



def build_E():
    nc = mknc()
    TT = TH
    NT = TT // 512

    S3B = nc.dram_tensor("S3B", [128, 8, TT], BF16, kind="ExternalInput").ap()
    WQ = nc.dram_tensor("WQ", [128, 8, 1024], F32, kind="ExternalInput").ap()
    WK = nc.dram_tensor("WK", [128, 8, 1024], F32, kind="ExternalInput").ap()
    WV = nc.dram_tensor("WV", [128, 8, 1024], F32, kind="ExternalInput").ap()

    q_o = nc.dram_tensor("q_o", [128, 8, TT], BF16, kind="ExternalOutput").ap()
    k_o = nc.dram_tensor("k_o", [128, 8, TT], BF16, kind="ExternalOutput").ap()
    v_o = nc.dram_tensor("v_o", [128, 16, 1024], BF16, kind="ExternalOutput").ap()

    with ExitStack() as ctx:
        s3b = sbuf(nc, ctx, "s3b", [128, 8, TT], BF16)
        wq = sbuf(nc, ctx, "wq", [128, 8, 1024], BF16)
        wk = sbuf(nc, ctx, "wk", [128, 8, 1024], BF16)
        wv = sbuf(nc, ctx, "wv", [128, 8, 1024], BF16)
        qt = [sbuf(nc, ctx, f"qt{i}", [128, 8, 512], BF16) for i in range(2)]
        kt = [sbuf(nc, ctx, f"kt{i}", [128, 8, 512], BF16) for i in range(2)]
        vt = [sbuf(nc, ctx, f"vt{i}", [128, 4, 1024], BF16) for i in range(2)]

        banks = [psum(nc, ctx, f"bank{i}", [128, 512], F32) for i in range(3)]

        p = Prog(nc, ctx)
        lg = p.sem("lg")
        smm = p.sem("smm")
        sac = p.sem("sac")
        stp = [p.sem("st0"), p.sem("st1")]
        rr = RR(p, banks)

        for sb_, dr in [(s3b, S3B), (wq, WQ), (wk, WK), (wv, WV)]:
            p.bump(lg, 16)
            p.on("gpsimd", lambda g, sb_=sb_, dr=dr: g.dma_start(out=sb_[:], in_=dr[:]).then_inc(lg, 16))
        lg_all = p.val(lg)

        store_done = [0, 0]
        ev_count = [0, 0]
        for it in range(NT):
            par = it % 2
            t0 = it * 512
            evs = 0
            for wsb, obuf in [(wq, qt[par]), (wk, kt[par])]:
                for ib in range(8):
                    bank, w = rr.acquire()
                    ew = list(w)
                    if it == 0 and ib == 0 and wsb is wq:
                        ew.append((lg, lg_all))
                    gv = emit_mm_group(
                        p, smm, bank,
                        [(lambda c=c, ib=ib, wsb=wsb: wsb[:, c, ib * 128:(ib + 1) * 128],
                          lambda c=c, t0=t0: s3b[:, c, t0:t0 + 512]) for c in range(8)],
                        extra_waits=ew)
                    p.wait("scalar", smm, gv)
                    if it >= 2 and ib == 0 and wsb is wq:
                        p.wait("scalar", stp[par], store_done[par])
                    p.on("scalar", lambda s, bank=bank, ib=ib, obuf=obuf:
                         s.activation(out=obuf[:, ib, :], in_=bank[:], func=AF.Copy).then_inc(sac, 1))
                    evs = p.bump(sac)
                    rr.release(sac, evs)
            for tb in range(4):
                for dv in range(2):
                    bank, w = rr.acquire()
                    gv = emit_mm_group(
                        p, smm, bank,
                        [(lambda c=c, t0=t0, tb=tb: s3b[:, c, t0 + tb * 128:t0 + tb * 128 + 128],
                          lambda c=c, dv=dv: wv[:, c, dv * 512:(dv + 1) * 512]) for c in range(8)],
                        extra_waits=w)
                    p.wait("scalar", smm, gv)
                    p.on("scalar", lambda s, bank=bank, tb=tb, dv=dv, par=par:
                         s.activation(out=vt[par][:, tb, dv * 512:(dv + 1) * 512], in_=bank[:],
                                      func=AF.Copy).then_inc(sac, 1))
                    evs = p.bump(sac)
                    rr.release(sac, evs)
            ev_count[par] = evs
            p.wait("sync", sac, evs)
            p.bump(stp[par], 48)
            p.on("sync", lambda s, t0=t0, par=par:
                 s.dma_start(out=q_o[:, :, t0:t0 + 512], in_=qt[par][:]).then_inc(stp[par], 16))
            p.on("sync", lambda s, t0=t0, par=par:
                 s.dma_start(out=k_o[:, :, t0:t0 + 512], in_=kt[par][:]).then_inc(stp[par], 16))
            p.on("sync", lambda s, it=it, par=par:
                 s.dma_start(out=v_o[:, it * 4:it * 4 + 4, :], in_=vt[par][:]).then_inc(stp[par], 16))
            store_done[par] = p.val(stp[par])

        p.wait("sync", stp[0], p.val(stp[0]))
        p.wait("sync", stp[1], p.val(stp[1]))
        with nc.Block() as block:
            p.finalize(block)
    return nc


def host_inputs_E(d_outs, inp):
    WQr = fm(inp["L1_attn_Wq"])
    WKr = fm(inp["L1_attn_Wk"])
    WVr = fm(inp["L1_attn_Wv"])
    maps = []
    for c in range(8):
        maps.append(dict(S3B=d_outs[c]["s3b_o"], WQ=WQr, WK=WKr, WV=WVr))
    return maps


def build_F():
    nc = mknc()
    NQT = T // 512

    QT = nc.dram_tensor("QT", [128, 4, T], BF16, kind="ExternalInput").ap()
    KT = nc.dram_tensor("KT", [128, 4, T], BF16, kind="ExternalInput").ap()
    VT = nc.dram_tensor("VT", [128, 32, 512], BF16, kind="ExternalInput").ap()
    WOR = nc.dram_tensor("WOR", [128, 4, 1024], F32, kind="ExternalInput").ap()
    SEL2 = nc.dram_tensor("SEL2", [128, 2], F32, kind="ExternalInput").ap()
    ONES1 = nc.dram_tensor("ONES1", [1, 128], F32, kind="ExternalInput").ap()
    ONES2 = nc.dram_tensor("ONES2", [2, 128], F32, kind="ExternalInput").ap()
    CC = nc.dram_tensor("CC", [128, NC], F32, kind="ExternalInput").ap()

    op_o = nc.dram_tensor("op_o", [128, 8, T], F32, kind="ExternalOutput").ap()
    nbd = nc.dram_tensor("nbd", [2, 4], F32, kind="Internal").ap()

    with ExitStack() as ctx:
        qts = sbuf(nc, ctx, "qts", [128, 4, T], BF16)
        kts = sbuf(nc, ctx, "kts", [128, 4, T], BF16)
        vts = sbuf(nc, ctx, "vts", [128, 32, 512], BF16)
        worb = sbuf(nc, ctx, "worb", [128, 4, 1024], BF16)
        sel2b = sbuf(nc, ctx, "sel2b", [128, 2], BF16)
        ones1f = sbuf(nc, ctx, "ones1f", [1, 128], F32)
        ones2f = sbuf(nc, ctx, "ones2f", [2, 128], F32)
        ccs = sbuf(nc, ctx, "ccs", [128, NC], F32)
        V65 = [sbuf(nc, ctx, f"V65_{j}", [128, 32, 65], BF16) for j in range(8)]
        sqs = sbuf(nc, ctx, "sqs", [128, 512], BF16)
        nrm_t = [sbuf(nc, ctx, f"nrm_t{i}", [2, 512], F32) for i in range(2)]
        tqmx = sbuf(nc, ctx, "tqmx", [2, 8], F32)
        tkmx = sbuf(nc, ctx, "tkmx", [2, 8], F32)
        qmx = sbuf(nc, ctx, "qmx", [2, 4], F32)
        kmx = sbuf(nc, ctx, "kmx", [2, 4], F32)
        m2s = sbuf(nc, ctx, "m2s", [2, 4], F32)
        z2 = sbuf(nc, ctx, "z2", [2, 8], F32)
        nb_sb = sbuf(nc, ctx, "nb_sb", [128, 8], F32)
        pp = [sbuf(nc, ctx, f"pp{i}", [128, 512], BF16) for i in range(2)]
        rec_sb = sbuf(nc, ctx, "rec_sb", [1, 512], F32)
        rb_sb = sbuf(nc, ctx, "rb_sb", [64, 512], F32)
        opt = [sbuf(nc, ctx, f"opt{i}", [128, 8, 512], F32) for i in range(2)]

        bA = [psum(nc, ctx, f"bA{i}", [128, 512], F32) for i in range(2)]
        bO = [psum(nc, ctx, f"bO{i}", [65, 512], F32) for i in range(2)]
        bB = [psum(nc, ctx, f"bB{i}", [128, 512], F32) for i in range(2)]

        p = Prog(nc, ctx)
        lg = p.sem("lg")
        ld = p.sem("ld")
        smm = p.sem("smm")
        sac = p.sem("sac")
        sdv = p.sem("sdv")
        sgp = p.sem("sgp")
        stp = [p.sem("st0"), p.sem("st1")]
        rrA = RR(p, bA)
        rrO = RR(p, bO)
        rrB = RR(p, bB)

        for sb_, dr in [(qts, QT), (kts, KT), (vts, VT)]:
            p.bump(ld, 16)
            p.on("sync", lambda s, sb_=sb_, dr=dr: s.dma_start(out=sb_[:], in_=dr[:]).then_inc(ld, 16))
        ld_qkv = p.val(ld)
        for sb_, dr in [(worb, WOR), (sel2b, SEL2), (ones1f, ONES1), (ones2f, ONES2), (ccs, CC)]:
            p.bump(lg, 16)
            p.on("gpsimd", lambda g, sb_=sb_, dr=dr: g.dma_start(out=sb_[:], in_=dr[:]).then_inc(lg, 16))
        lg_all = p.val(lg)

        p.wait("vector", ld, ld_qkv)
        for j in range(8):
            p.on("vector", lambda v, j=j: v.memset(V65[j][:, :, 64:65], 1.0))
            for tb in range(32):
                p.on("vector", lambda v, j=j, tb=tb:
                     v.tensor_copy(V65[j][:, tb, 0:64], vts[:, tb, j * 64:(j + 1) * 64]))
        p.on("vector", lambda v: v.tensor_copy(sqs[:, 0:64], vts[:, 0, 0:64]).then_inc(sdv, 1))
        v65_done = p.bump(sdv)

        p.wait("scalar", ld, ld_qkv)
        p.wait("scalar", lg, lg_all)
        for cj in range(4):
            for tl in range(8):
                for which, src, tmx in [(0, qts, tqmx), (1, kts, tkmx)]:
                    p.on("scalar", lambda s, src=src, cj=cj, tl=tl:
                         s.activation(out=sqs[:], in_=src[:, cj, tl * 512:(tl + 1) * 512],
                                      func=AF.Square).then_inc(sac, 1))
                    sqv = p.bump(sac)
                    bank, w = rrB.acquire()
                    gv = emit_mm_group(p, smm, bank,
                                       [(lambda: sel2b[:], lambda: sqs[:])],
                                       extra_waits=w + [(sac, sqv)], rows=2)
                    p.wait("scalar", smm, gv)
                    p.on("scalar", lambda s, bank=bank, which=which:
                         s.activation(out=nrm_t[which][:], in_=bank[0:2, :],
                                      func=AF.Copy).then_inc(sac, 1))
                    ev = p.bump(sac)
                    rrB.release(sac, ev)
                    p.wait("vector", sac, ev)
                    p.on("vector", lambda v, which=which, tmx=tmx, tl=tl:
                         v.reduce_max(tmx[:, tl:tl + 1], nrm_t[which][:],
                                      axis=mybir.AxisListType.X))
            p.on("vector", lambda v: v.tensor_copy(sqs[:, 0:8], sqs[:, 8:16]))
            p.on("vector", lambda v: v.tensor_copy(sqs[:, 0:8], sqs[:, 8:16]))
            p.on("vector", lambda v, cj=cj:
                 v.reduce_max(qmx[:, cj:cj + 1], tqmx[:], axis=mybir.AxisListType.X))
            p.on("vector", lambda v, cj=cj:
                 v.reduce_max(kmx[:, cj:cj + 1], tkmx[:], axis=mybir.AxisListType.X))
        p.on("vector", lambda v: v.memset(z2[:], 0.0))
        p.on("vector", lambda v: v.tensor_copy(sqs[:, 0:8], sqs[:, 8:16]))
        p.on("vector", lambda v: v.tensor_mul(m2s[:], qmx[:], kmx[:]).then_inc(sdv, 1))
        m2_done = p.bump(sdv)
        p.wait("scalar", sdv, m2_done)
        p.on("scalar", lambda s: s.activation(out=m2s[:], in_=m2s[:], func=AF.Sqrt).then_inc(sac, 1))
        sqrt_done = p.bump(sac)
        p.wait("vector", sac, sqrt_done)
        p.on("vector", lambda v: v.tensor_scalar_mul(m2s[:], m2s[:], -BETA).then_inc(sdv, 1))
        z2_done = p.bump(sdv)
        p.wait("sync", sdv, z2_done)
        p.bump(stp[0], 16)
        p.on("sync", lambda s: s.dma_start(out=nbd[:], in_=m2s[:]).then_inc(stp[0], 16))
        nbd_w = p.val(stp[0])
        p.wait("sync", stp[0], nbd_w)
        p.bump(stp[0], 16)
        p.on("sync", lambda s: s.dma_start(out=z2[0:1, :].rearrange("p (c r) -> p c r", c=4, r=2),
                                           in_=nbd.rearrange("r c -> c r"),
                                           allow_slow_non_contiguous=True).then_inc(stp[0], 16))
        nbr_done = p.val(stp[0])
        bankN, wN = rrB.acquire()
        gvN = emit_mm_group(p, smm, bankN, [(lambda: ones1f[:], lambda: z2[0:1, :])],
                            extra_waits=wN + [(stp[0], nbr_done)], width=8)
        p.wait("scalar", smm, gvN)
        p.on("scalar", lambda s, bankN=bankN:
             s.activation(out=nb_sb[:], in_=bankN[:, 0:8], func=AF.Copy).then_inc(sac, 1))
        nb_done = p.bump(sac)
        rrB.release(sac, nb_done)

        pv_done = {}
        o_written = {}
        for j in range(8):
            base = (j % 2) * 64
            cj = j // 2
            for qt in range(NQT):
                nkb = 4 * qt + 4
                bankO, wO = rrO.acquire()
                for kb in range(nkb):
                    sl = kb % 2
                    bankS, wS = rrA.acquire()
                    ew = list(wS)
                    if j == 0 and qt == 0 and kb == 0:
                        ew += [(ld, ld_qkv)]
                    gv = emit_mm_group(
                        p, smm, bankS,
                        [(lambda j=j, cj=cj, kb=kb, base=base: kts[base:base + 64, cj, kb * 128:(kb + 1) * 128],
                          lambda j=j, cj=cj, qt=qt, base=base: qts[base:base + 64, cj, qt * 512:(qt + 1) * 512])],
                        extra_waits=ew)
                    p.wait("scalar", smm, gv)
                    if (j, qt, kb) == (0, 0, 0):
                        p.wait("scalar", sac, nb_done)
                    pvk = pv_done.get((j, qt, kb - 2))
                    if pvk is None and kb < 2:
                        pvk = pv_done.get(("last", sl))
                    if pvk:
                        p.wait("scalar", smm, pvk)
                    p.on("scalar", lambda s, bankS=bankS, sl=sl, j=j:
                         s.activation(out=pp[sl][:], in_=bankS[:], func=AF.Exp,
                                      scale=ccs[:, C_BETA:C_BETA + 1],
                                      bias=nb_sb[:, j:j + 1]).then_inc(sac, 1))
                    expv = p.bump(sac)
                    rrA.release(sac, expv)
                    diag = (kb * 128) >= (qt * 512) and (kb * 128) < (qt + 1) * 512
                    gate_sem, gate_val = sac, expv
                    if diag:
                        p.wait("gpsimd", sac, expv)
                        p.on("gpsimd", lambda g, sl=sl, qt=qt, kb=kb:
                             g.affine_select(out=pp[sl][:], in_=pp[sl][:], pattern=[[1, 512]],
                                             compare_op=ALU.is_ge, fill=0.0,
                                             base=qt * 512 - kb * 128,
                                             channel_multiplier=-1).then_inc(sgp, 1))
                        gate_sem, gate_val = sgp, p.bump(sgp)
                    p.wait("tensor", gate_sem, gate_val)
                    if kb == 0:
                        for hh_, vv in wO:
                            p.wait("tensor", hh_, vv)
                        p.wait("tensor", sdv, v65_done)
                    mv = p.bump(smm)
                    p.on("tensor", lambda t, bankO=bankO, j=j, kb=kb, sl=sl, nkb=nkb:
                         t.matmul(bankO[:], V65[j][:, kb, :], pp[sl][:],
                                  start=(kb == 0), stop=(kb == nkb - 1)).then_inc(smm, 1))
                    pv_done[(j, qt, kb)] = mv
                pv_done[("last", (nkb - 1) % 2)] = mv
                pv_done[("last", nkb % 2)] = pv_done.get((j, qt, nkb - 2), mv)
                p.wait("vector", smm, mv)
                p.on("vector", lambda v, bankO=bankO:
                     v.reciprocal(rec_sb[:], bankO[64:65, :]).then_inc(sdv, 1))
                rec_done = p.bump(sdv)
                bankR, wR = rrB.acquire()
                gvR = emit_mm_group(p, smm, bankR,
                                    [(lambda: ones1f[:, 0:64], lambda: rec_sb[:])],
                                    extra_waits=wR + [(sdv, rec_done)], rows=64)
                p.wait("vector", smm, gvR)
                p.on("vector", lambda v, bankR=bankR: v.tensor_copy(rb_sb[:], bankR[0:64, :]))
                p.on("vector", lambda v, bankO=bankO, base=base, cj=cj, qt=qt:
                     v.tensor_mul(vts[base:base + 64, cj * 8 + qt, :], bankO[0:64, :],
                                  rb_sb[:]).then_inc(sdv, 1))
                ov = p.bump(sdv)
                o_written[(j, qt)] = ov
                rrB.release(sdv, ov)
                rrO.release(sdv, ov)

        store_done = [0, 0]
        for qt in range(NQT):
            par = qt % 2
            for dmb in range(8):
                bank, w = rrA.acquire()
                ew = list(w)
                if dmb == 0:
                    ew += [(sdv, max(o_written[(j, qt)] for j in range(8)))]
                gv = emit_mm_group(
                    p, smm, bank,
                    [(lambda ic=ic, dmb=dmb: worb[:, ic, dmb * 128:(dmb + 1) * 128],
                      lambda ic=ic, qt=qt: vts[:, ic * 8 + qt, :]) for ic in range(4)],
                    extra_waits=ew)
                p.wait("scalar", smm, gv)
                if dmb == 0 and qt >= 2:
                    p.wait("scalar", stp[par], store_done[par])
                p.on("scalar", lambda s, bank=bank, dmb=dmb, par=par:
                     s.activation(out=opt[par][:, dmb, :], in_=bank[:], func=AF.Copy).then_inc(sac, 1))
                rrA.release(sac, p.bump(sac))
            p.wait("sync", sac, p.val(sac))
            p.bump(stp[par], 16)
            p.on("sync", lambda s, qt=qt, par=par:
                 s.dma_start(out=op_o[:, :, qt * 512:(qt + 1) * 512], in_=opt[par][:]).then_inc(stp[par], 16))
            store_done[par] = p.val(stp[par])

        p.wait("sync", stp[0], p.val(stp[0]))
        p.wait("sync", stp[1], p.val(stp[1]))
        with nc.Block() as block:
            p.finalize(block)
    return nc


def host_inputs_F(e_outs, inp):
    cc = host_consts()
    sel2 = np.zeros((128, 2), np.float32)
    sel2[0:64, 0] = 1.0
    sel2[64:128, 1] = 1.0
    ones1 = np.ones((1, 128), np.float32)
    ones2 = np.ones((2, 128), np.float32)
    Wo = inp["L1_attn_Wo"]
    maps = []
    for c in range(8):
        b, hh_ = c // 2, c % 2
        e0, e1 = e_outs[2 * b], e_outs[2 * b + 1]
        qf = np.concatenate([e0["q_o"], e1["q_o"]], axis=2)
        kf = np.concatenate([e0["k_o"], e1["k_o"]], axis=2)
        vf = np.concatenate([e0["v_o"], e1["v_o"]], axis=1)
        csl = slice(hh_ * 4, hh_ * 4 + 4)
        wor = fm(np.ascontiguousarray(Wo[hh_ * 512:(hh_ + 1) * 512]))
        maps.append(dict(QT=np.ascontiguousarray(qf[:, csl]),
                         KT=np.ascontiguousarray(kf[:, csl]),
                         VT=np.ascontiguousarray(vf[:, :, hh_ * 512:(hh_ + 1) * 512]),
                         WOR=wor, SEL2=sel2, ONES1=ones1, ONES2=ones2, CC=cc))
    return maps


def build_G():
    nc = mknc()
    TT = TH
    NT = TT // 512
    NTB = TT // 128

    P0 = nc.dram_tensor("P0", [128, 8, TT], F32, kind="ExternalInput").ap()
    P1 = nc.dram_tensor("P1", [128, 8, TT], F32, kind="ExternalInput").ap()
    S3F = nc.dram_tensor("S3F", [128, 8, TT], F32, kind="ExternalInput").ap()
    BOC = nc.dram_tensor("BOC", [128, 8], F32, kind="ExternalInput").ap()
    GWT = nc.dram_tensor("GWT", [128, 8, 16], F32, kind="ExternalInput").ap()
    EUH = nc.dram_tensor("EUH", [16, 128, 8, 512], F32, kind="ExternalInput").ap()
    EDH = nc.dram_tensor("EDH", [16, 128, 4, 1024], F32, kind="ExternalInput").ap()
    SUH = nc.dram_tensor("SUH", [128, 8, 2048], F32, kind="ExternalInput").ap()
    SDH = nc.dram_tensor("SDH", [128, 16, 1024], F32, kind="ExternalInput").ap()
    OH16 = nc.dram_tensor("OH16", [16, 16 * 128], F32, kind="ExternalInput").ap()
    IDT = nc.dram_tensor("IDT", [128, 128], F32, kind="ExternalInput").ap()
    ONES128 = nc.dram_tensor("ONES128", [128, 1], F32, kind="ExternalInput").ap()
    ONES1 = nc.dram_tensor("ONES1", [1, 128], F32, kind="ExternalInput").ap()
    CC = nc.dram_tensor("CC", [128, NC], F32, kind="ExternalInput").ap()

    s5_o = nc.dram_tensor("s5_o", [128, 8, TT], F32, kind="ExternalOutput").ap()
    s4d = nc.dram_tensor("s4d", [128, 8, TT], F32, kind="Internal").ap()

    with ExitStack() as ctx:
        macc = sbuf(nc, ctx, "macc", [128, 8, TT], F32)
        s4b = sbuf(nc, ctx, "s4b", [128, 8, TT], BF16)
        wbuf = [sbuf(nc, ctx, f"wbuf{i}", [128, 8192], BF16) for i in range(2)]
        su = sbuf(nc, ctx, "su", [128, 8, 512], BF16)
        cbc = sbuf(nc, ctx, "cbc", [128, 512], BF16)
        p0t = sbuf(nc, ctx, "p0t", [128, 8, 256], F32)
        p1t = sbuf(nc, ctx, "p1t", [128, 8, 256], F32)
        of = sbuf(nc, ctx, "of", [128, 8, 256], F32)
        s1b = sbuf(nc, ctx, "s1b", [128, 8, 256], BF16)
        sqb = sbuf(nc, ctx, "sqb", [128, 8, 256], BF16)
        m_sb = sbuf(nc, ctx, "m_sb", [1, 512], F32)
        r_sb = sbuf(nc, ctx, "r_sb", [1, 512], F32)
        t1g = sbuf(nc, ctx, "t1g", [1, 512], F32)
        gwt = sbuf(nc, ctx, "gwt", [128, 8, 16], F32)
        boc = sbuf(nc, ctx, "boc", [128, 8], F32)
        ccs = sbuf(nc, ctx, "ccs", [128, NC], F32)
        ones128b = sbuf(nc, ctx, "ones128b", [128, 1], BF16)
        ones1f = sbuf(nc, ctx, "ones1f", [1, 128], F32)
        idt = sbuf(nc, ctx, "idt", [128, 128], F32)
        oh16b = sbuf(nc, ctx, "oh16b", [16, 16 * 128], BF16)
        scT = sbuf(nc, ctx, "scT", [16, TT], F32)
        sc_tok = sbuf(nc, ctx, "sc_tok", [128, NTB, 16], F32)
        comb = sbuf(nc, ctx, "comb", [128, NTB, 16], F32)
        combTb = sbuf(nc, ctx, "combTb", [16, TT], BF16)
        r4 = [sbuf(nc, ctx, f"r4_{i}", [128, 4], F32) for i in range(2)]
        r4b = [sbuf(nc, ctx, f"r4b_{i}", [128, 4], F32) for i in range(2)]
        r16 = [sbuf(nc, ctx, f"r16_{i}", [128, 16], F32) for i in range(2)]
        r16b = [sbuf(nc, ctx, f"r16b_{i}", [128, 16], F32) for i in range(2)]
        r16c = [sbuf(nc, ctx, f"r16c_{i}", [128, 16], F32) for i in range(2)]
        r1 = [sbuf(nc, ctx, f"r1_{i}", [128, 1], F32) for i in range(2)]
        r1b = [sbuf(nc, ctx, f"r1b_{i}", [128, 1], F32) for i in range(2)]
        gsum = [sbuf(nc, ctx, f"gsum_{i}", [128, 4], F32) for i in range(2)]

        bU = [psum(nc, ctx, f"bU{i}", [128, 512], F32) for i in range(2)]
        bD = [psum(nc, ctx, f"bD{i}", [128, 512], F32) for i in range(2)]
        bM = [psum(nc, ctx, f"bM{i}", [128, 512], F32) for i in range(2)]
        bL = [psum(nc, ctx, f"bL{i}", [128, 512], F32) for i in range(2)]

        p = Prog(nc, ctx)
        lg = p.sem("lg")
        lwp = [p.sem("lw0"), p.sem("lw1")]
        ldt = p.sem("ldt")
        smm = p.sem("smm")
        sac = p.sem("sac")
        sdv = p.sem("sdv")
        stp = [p.sem("st0"), p.sem("st1")]
        rrU = RR(p, bU)
        rrD = RR(p, bD)
        rrM = RR(p, bM)
        rrL = RR(p, bL)
        env = dict(s1b=s1b, sqb=sqb, m_sb=m_sb, r_sb=r_sb, t1=t1g, df=p0t,
                   ones128b=ones128b, ones1f=ones1f, ccs=ccs,
                   rrst=rrL, rrbc2=rrM, smm=smm, sac=sac, sdv=sdv)

        for sb_, dr in [(gwt, GWT), (boc, BOC), (ccs, CC), (ones1f, ONES1),
                        (idt, IDT), (oh16b, OH16)]:
            p.bump(lg, 16)
            p.on("gpsimd", lambda g, sb_=sb_, dr=dr: g.dma_start(out=sb_[:], in_=dr[:]).then_inc(lg, 16))
        p.bump(lg, 16)
        p.on("gpsimd", lambda g: g.dma_start(out=ones128b[:], in_=ONES128[:]).then_inc(lg, 16))
        lg_all = p.val(lg)

        NT1 = TT // 256
        ln_last = 0
        rt_done = [0] * NT1
        for it in range(NT1):
            t0 = it * 256
            if it > 0:
                p.wait("sync", sdv, ln_last)
            p.bump(ldt, 32)
            p.on("sync", lambda s, t0=t0: s.dma_start(out=p0t[:], in_=P0[:, :, t0:t0 + 256]).then_inc(ldt, 16))
            p.on("sync", lambda s, t0=t0: s.dma_start(out=p1t[:], in_=P1[:, :, t0:t0 + 256]).then_inc(ldt, 16))
            ld1 = p.val(ldt)
            p.wait("vector", ldt, ld1)
            p.on("vector", lambda v: v.tensor_add(p0t[:], p0t[:], p1t[:]).then_inc(sdv, 1))
            add1 = p.bump(sdv)
            p.wait("sync", sdv, add1)
            p.bump(ldt, 16)
            p.on("sync", lambda s, t0=t0: s.dma_start(out=p1t[:], in_=S3F[:, :, t0:t0 + 256]).then_inc(ldt, 16))
            ld2 = p.val(ldt)
            p.wait("scalar", sdv, add1)
            p.wait("scalar", lg, lg_all)
            for c in range(8):
                p.on("scalar", lambda s, c=c:
                     s.activation(out=p0t[:, c, :], in_=p0t[:, c, :], func=AF.Identity,
                                  bias=boc[:, c:c + 1]).then_inc(sac, 1))
                p.bump(sac)
            bias_done = p.val(sac)
            p.wait("vector", sac, bias_done)
            p.wait("vector", ldt, ld2)
            p.on("vector", lambda v: v.tensor_add(p0t[:], p0t[:], p1t[:]).then_inc(sdv, 1))
            res_done = p.bump(sdv)
            ln_last = emit_ln(
                p, env, lambda: p0t[:],
                (lambda: of[:], lambda t0=t0: s4b[:, :, t0:t0 + 256]),
                256, waits=[(sdv, res_done)] + ([(stp[0], st_s4)] if it > 0 else []))
            p.wait("sync", sdv, ln_last)
            p.bump(stp[0], 16)
            p.on("sync", lambda s, t0=t0: s.dma_start(out=s4d[:, :, t0:t0 + 256], in_=of[:]).then_inc(stp[0], 16))
            st_s4 = p.val(stp[0])
            bank, w = rrM.acquire()
            gv = emit_mm_group(p, smm, bank,
                               [(lambda c=c: gwt[:, c, :], lambda c=c: of[:, c, :]) for c in range(8)],
                               extra_waits=w + [(sdv, ln_last)], rows=16, width=256)
            p.wait("scalar", smm, gv)
            p.on("scalar", lambda s, bank=bank, t0=t0:
                 s.activation(out=scT[:, t0:t0 + 256], in_=bank[0:16, 0:256], func=AF.Sigmoid).then_inc(sac, 1))
            rt_done[it] = p.bump(sac)
            rrM.release(sac, rt_done[it])

        tr_done = [0] * NTB
        for tb in range(NTB):
            bank, w = rrM.acquire()
            for hh_, vv in w:
                p.wait("tensor", hh_, vv)
            p.wait("tensor", sac, rt_done[NT1 - 1])
            mv = p.bump(smm)
            p.on("tensor", lambda t, bank=bank, tb=tb:
                 t.transpose(bank[:, 0:16], scT[:, tb * 128:(tb + 1) * 128], idt[0:16, 0:16]).then_inc(smm, 1))
            p.wait("scalar", smm, mv)
            p.on("scalar", lambda s, bank=bank, tb=tb:
                 s.activation(out=sc_tok[:, tb, :], in_=bank[:, 0:16], func=AF.Copy).then_inc(sac, 1))
            tr_done[tb] = p.bump(sac)
            rrM.release(sac, tr_done[tb])

        BIG = -1.0e30
        def route_steps(tb, z):
            sc = lambda: sc_tok[:, tb, :]
            steps = []
            for g in range(4):
                steps.append(lambda v, g=g: v.reduce_max(r1[z][:], sc()[:, 4 * g:4 * g + 4],
                                                         axis=mybir.AxisListType.X))
                steps.append(lambda v, g=g: v.tensor_scalar(out=r4[z][:], in0=sc()[:, 4 * g:4 * g + 4],
                                                            scalar1=r1[z][:], scalar2=None,
                                                            op0=ALU.subtract))
                steps.append(lambda v, g=g: v.tensor_single_scalar(r4b[z][:], r4[z][:], 0.0, ALU.is_ge))
                steps.append(lambda v, g=g: v.scalar_tensor_tensor(out=r4[z][:], in0=r4b[z][:],
                                                                   scalar=BIG, in1=sc()[:, 4 * g:4 * g + 4],
                                                                   op0=ALU.mult, op1=ALU.add))
                steps.append(lambda v, g=g: v.reduce_max(r1b[z][:], r4[z][:],
                                                         axis=mybir.AxisListType.X))
                steps.append(lambda v, g=g: v.tensor_add(gsum[z][:, g:g + 1], r1[z][:], r1b[z][:]))
            steps.append(lambda v: v.reduce_max(r1[z][:], gsum[z][:], axis=mybir.AxisListType.X))
            steps.append(lambda v: v.tensor_scalar(out=r4[z][:], in0=gsum[z][:], scalar1=r1[z][:],
                                                   scalar2=None, op0=ALU.subtract))
            steps.append(lambda v: v.tensor_single_scalar(r4b[z][:], r4[z][:], 0.0, ALU.is_ge))
            steps.append(lambda v: v.scalar_tensor_tensor(out=r4[z][:], in0=r4b[z][:], scalar=BIG,
                                                          in1=gsum[z][:], op0=ALU.mult, op1=ALU.add))
            steps.append(lambda v: v.reduce_max(r1b[z][:], r4[z][:], axis=mybir.AxisListType.X))
            steps.append(lambda v: v.tensor_scalar(out=r4[z][:], in0=gsum[z][:], scalar1=r1b[z][:],
                                                   scalar2=None, op0=ALU.is_ge))
            for g in range(4):
                steps.append(lambda v, g=g: v.tensor_scalar(out=r16[z][:, 4 * g:4 * g + 4],
                                                            in0=sc()[:, 4 * g:4 * g + 4],
                                                            scalar1=r4[z][:, g:g + 1], scalar2=None,
                                                            op0=ALU.mult))
            steps.append(lambda v: v.tensor_copy(r16b[z][:], r16[z][:]))
            for rrm in range(3):
                steps.append(lambda v: v.reduce_max(r1[z][:], r16b[z][:], axis=mybir.AxisListType.X))
                steps.append(lambda v: v.tensor_scalar(out=r4b[z][:], in0=r4[z][:],
                                                       scalar1=0.0, scalar2=None, op0=ALU.mult))
                steps.append(lambda v: v.tensor_scalar(out=r16c[z][:], in0=r16b[z][:],
                                                       scalar1=r1[z][:], scalar2=None,
                                                       op0=ALU.is_lt))
                steps.append(lambda v: v.tensor_mul(r16b[z][:], r16b[z][:], r16c[z][:]))
            steps.append(lambda v: v.reduce_max(r1[z][:], r16b[z][:], axis=mybir.AxisListType.X))
            steps.append(lambda v: v.tensor_copy(r4[z][:, 1:2], r1[z][:]))
            steps.append(lambda v: v.tensor_scalar(out=r16b[z][:], in0=r16[z][:],
                                                   scalar1=r4[z][:, 1:2], scalar2=None, op0=ALU.is_ge))
            steps.append(lambda v: v.tensor_mul(r16b[z][:], r16b[z][:], r16[z][:]))
            steps.append(lambda v: v.reduce_sum(r1[z][:], r16b[z][:], axis=mybir.AxisListType.X))
            steps.append(lambda v: v.tensor_scalar(out=r1b[z][:], in0=r1[z][:], scalar1=1e-20,
                                                   scalar2=None, op0=ALU.add))
            steps.append(lambda v: v.reciprocal(r1[z][:], r1b[z][:]))
            steps.append(lambda v: v.tensor_copy(r1b[z][:], r1[z][:]))
            steps.append(lambda v, tb=tb: v.tensor_scalar(out=comb[:, tb, :], in0=r16b[z][:],
                                                          scalar1=r1b[z][:], scalar2=2.5,
                                                          op0=ALU.mult, op1=ALU.mult))
            return steps

        p.wait("vector", sac, max(tr_done))
        for tp in range(0, NTB, 2):
            sa = route_steps(tp, 0)
            sb2 = route_steps(tp + 1, 1)
            n = max(len(sa), len(sb2))
            for i in range(n):
                if i < len(sa):
                    p.on("vector", sa[i])
                if i < len(sb2):
                    p.on("vector", sb2[i])
        p.on("vector", lambda v: v.tensor_copy(r16[0][:], comb[:, 0, :]))
        p.on("vector", lambda v: v.tensor_copy(r16[1][:], comb[:, 1, :]).then_inc(sdv, 1))
        route_done = p.bump(sdv)

        ct_done = 0
        for tb in range(NTB):
            bank, w = rrM.acquire()
            for hh_, vv in w:
                p.wait("tensor", hh_, vv)
            if tb == 0:
                p.wait("tensor", sdv, route_done)
            mv = p.bump(smm)
            p.on("tensor", lambda t, bank=bank, tb=tb:
                 t.transpose(bank[:, 0:128][0:16, :], comb[:, tb, :], idt[:]).then_inc(smm, 1))
            p.wait("scalar", smm, mv)
            p.on("scalar", lambda s, bank=bank, tb=tb:
                 s.activation(out=combTb[:, tb * 128:(tb + 1) * 128], in_=bank[0:16, 0:128],
                              func=AF.Copy).then_inc(sac, 1))
            ct_done = p.bump(sac)
            rrM.release(sac, ct_done)
        p.wait("vector", sac, ct_done)
        p.on("vector", lambda v: v.tensor_copy(r16[0][:], comb[:, 0, :]).then_inc(sdv, 1))
        combT_done = p.bump(sdv)
        p.wait("tensor", sac, ct_done)

        wb_flat = [wbuf[i][:].rearrange("p (a b) -> p a b", a=8, b=1024) for i in range(2)]

        mc_first = {}
        mc_done = {}
        last_scale = {"v": 0}
        last_dn = {"gv": 0}
        last_mm_on_slot = {0: 0, 1: 0}

        def emit_pass(slot_up, slot_dn, n_ib, n_ic, up_lhs, dn_lhs, e, ld_wait):
            for tti in range(NT):
                su_ev = 0
                for ib in range(n_ib):
                    bank, w = rrU.acquire()
                    ew = list(w)
                    if ib == 0 and tti == 0:
                        ew.extend(ld_wait)
                        ew.append((sdv, combT_done))
                    gv = emit_mm_group(
                        p, smm, bank,
                        [(lambda c=c, ib=ib: up_lhs(c, ib),
                          lambda c=c, tti=tti: s4b[:, c, tti * 512:(tti + 1) * 512]) for c in range(8)],
                        extra_waits=ew)
                    last_mm_on_slot[slot_up] = gv
                    p.wait("scalar", smm, gv)
                    if ib == 0:
                        p.wait("scalar", smm, last_dn["gv"])
                    p.on("scalar", lambda s, bank=bank, ib=ib:
                         s.activation(out=su[:, ib, :], in_=bank[:], func=AF.Silu).then_inc(sac, 1))
                    su_ev = p.bump(sac)
                    rrU.release(sac, su_ev)
                scale_done = None
                if e is not None:
                    bank, w = rrM.acquire()
                    gv = emit_mm_group(
                        p, smm, bank,
                        [(lambda e=e: oh16b[:, e * 128:(e + 1) * 128],
                          lambda tti=tti: combTb[0:16, tti * 512:(tti + 1) * 512])],
                        extra_waits=w)
                    p.wait("scalar", smm, gv)
                    if last_scale["v"]:
                        p.wait("scalar", sdv, last_scale["v"])
                    p.on("scalar", lambda s, bank=bank:
                         s.activation(out=cbc[:], in_=bank[:], func=AF.Copy).then_inc(sac, 1))
                    cb_ev = p.bump(sac)
                    rrM.release(sac, cb_ev)
                    p.wait("vector", sac, max(su_ev, cb_ev))
                    for ib in range(n_ib):
                        last = ib == n_ib - 1
                        if last:
                            p.on("vector", lambda v, ib=ib:
                                 v.tensor_mul(su[:, ib, :], su[:, ib, :], cbc[:]).then_inc(sdv, 1))
                        else:
                            p.on("vector", lambda v, ib=ib:
                                 v.tensor_mul(su[:, ib, :], su[:, ib, :], cbc[:]))
                    scale_done = p.bump(sdv)
                    last_scale["v"] = scale_done
                for dmb in range(8):
                    bank, w = rrD.acquire()
                    ew = list(w)
                    if dmb == 0:
                        if scale_done is not None:
                            ew.append((sdv, scale_done))
                        else:
                            ew.append((sac, su_ev))
                    gv = emit_mm_group(
                        p, smm, bank,
                        [(lambda ic=ic, dmb=dmb: dn_lhs(ic, dmb),
                          lambda ic=ic: su[:, ic, :]) for ic in range(n_ic)],
                        extra_waits=ew)
                    last_dn["gv"] = gv
                    last_mm_on_slot[slot_dn] = gv
                    key = (tti, dmb)
                    first = key not in mc_first
                    mc_first[key] = True
                    p.wait("vector", smm, gv)
                    if first:
                        p.on("vector", lambda v, bank=bank, tti=tti, dmb=dmb:
                             v.tensor_copy(macc[:, dmb, tti * 512:(tti + 1) * 512], bank[:]).then_inc(sdv, 1))
                    else:
                        p.on("vector", lambda v, bank=bank, tti=tti, dmb=dmb:
                             v.tensor_add(macc[:, dmb, tti * 512:(tti + 1) * 512],
                                          macc[:, dmb, tti * 512:(tti + 1) * 512], bank[:]).then_inc(sdv, 1))
                    mc_done[key] = p.bump(sdv)
                    rrD.release(sdv, mc_done[key])

        lw_val = {0: 0, 1: 0}

        def emit_wload(slot, srcs, wait_mm):
            if wait_mm:
                p.wait("gpsimd", smm, wait_mm)
            for dst, src in srcs:
                p.bump(lwp[slot], 16)
                p.on("gpsimd", lambda g, dst=dst, src=src: g.dma_start(out=dst, in_=src).then_inc(lwp[slot], 16))
            lw_val[slot] = p.val(lwp[slot])
            return (lwp[slot], lw_val[slot])

        ldw = emit_wload(0, [(wb_flat[0][:], SUH[:, :, 0:1024])], 0)
        ldw1 = emit_wload(1, [(wb_flat[1][:], SDH[:, 0:8, :])], 0)
        emit_pass(0, 1, 8, 8,
                  lambda c, ib: wb_flat[0][:, c, ib * 128:(ib + 1) * 128],
                  lambda ic, dmb: wb_flat[1][:, ic, dmb * 128:(dmb + 1) * 128],
                  None, [(lwp[0], ldw[1]), (lwp[1], ldw1[1])])
        ldw = emit_wload(0, [(wb_flat[0][:], SUH[:, :, 1024:2048])],
                         last_mm_on_slot[0])
        ldw1 = emit_wload(1, [(wb_flat[1][:], SDH[:, 8:16, :])],
                          last_mm_on_slot[1])
        emit_pass(0, 1, 8, 8,
                  lambda c, ib: wb_flat[0][:, c, ib * 128:(ib + 1) * 128],
                  lambda ic, dmb: wb_flat[1][:, ic, dmb * 128:(dmb + 1) * 128],
                  None, [(lwp[0], ldw[1]), (lwp[1], ldw1[1])])
        for e in range(NEXP):
            slot = e % 2
            ldw = emit_wload(slot, [
                (wbuf[slot][:, 0:4096].rearrange("p (a b) -> p a b", a=8, b=512), EUH[e]),
                (wbuf[slot][:, 4096:8192].rearrange("p (a b) -> p a b", a=4, b=1024), EDH[e]),
            ], last_mm_on_slot[slot])
            flat = wbuf[slot]
            emit_pass(slot, slot, 4, 4,
                      lambda c, ib, flat=flat: flat[:, c * 512 + ib * 128:c * 512 + (ib + 1) * 128],
                      lambda ic, dmb, flat=flat: flat[:, 4096 + ic * 1024 + dmb * 128:4096 + ic * 1024 + (dmb + 1) * 128],
                      e, [ldw])

        import os as _os
        SKIP4 = _os.environ.get("G_SKIP4") == "1"
        st_out = 0
        env4 = dict(env)
        env4["df"] = p0t
        for it in range(NT1 if not SKIP4 else 0):
            t0 = it * 256
            tt_of = t0 // 512
            p.wait("sync", sdv, mc_done[(tt_of, 7)])
            if it > 0:
                p.wait("sync", sdv, ln_last)
            p.bump(ldt, 16)
            p.on("sync", lambda s, t0=t0: s.dma_start(out=p0t[:], in_=s4d[:, :, t0:t0 + 256]).then_inc(ldt, 16))
            ld4 = p.val(ldt)
            p.wait("vector", ldt, ld4)
            mwait = max(mc_done[(tt_of, dmb)] for dmb in range(8))
            p.wait("vector", sdv, mwait)
            for c in range(8):
                if c == 7:
                    p.on("vector", lambda v, c=c, t0=t0:
                         v.tensor_add(p0t[:, c, :], p0t[:, c, :], macc[:, c, t0:t0 + 256]).then_inc(sdv, 1))
                else:
                    p.on("vector", lambda v, c=c, t0=t0:
                         v.tensor_add(p0t[:, c, :], p0t[:, c, :], macc[:, c, t0:t0 + 256]))
            res4 = p.bump(sdv)
            lnw = [(sdv, res4)]
            if it > 0:
                lnw.append((stp[1], st_out))
            ln_last = emit_ln(p, env4, lambda: p0t[:], (lambda: of[:], None), 256, waits=lnw)
            p.wait("sync", sdv, ln_last)
            p.bump(stp[1], 16)
            p.on("sync", lambda s, t0=t0: s.dma_start(out=s5_o[:, :, t0:t0 + 256], in_=of[:]).then_inc(stp[1], 16))
            st_out = p.val(stp[1])

        if SKIP4:
            p.wait("sync", sdv, mc_done[(NT - 1, 7)])
            p.bump(stp[1], 16)
            p.on("sync", lambda s: s.dma_start(out=s5_o[:], in_=macc[:]).then_inc(stp[1], 16))
        p.wait("sync", stp[0], p.val(stp[0]))
        p.wait("sync", stp[1], p.val(stp[1]))
        p.wait("sync", lwp[0], p.val(lwp[0]))
        p.wait("sync", lwp[1], p.val(lwp[1]))
        with nc.Block() as block:
            p.finalize(block)
    return nc


def host_inputs_G(f_outs, d_outs, inp):
    cc = host_consts()
    ones128 = np.ones((128, 1), np.float32)
    ones1 = np.ones((1, 128), np.float32)
    idt = np.eye(128, dtype=np.float32)
    oh = np.zeros((16, 16 * 128), np.float32)
    for e in range(16):
        oh[e, e * 128:(e + 1) * 128] = 1.0
    gwt = fm(np.ascontiguousarray(inp["L1_ffn_gate_weight"].T))
    boc = cols(inp["L1_attn_bo"])
    eu = inp["L1_ffn_expert_up"]
    ed = inp["L1_ffn_expert_down"]
    euh = np.ascontiguousarray(eu.reshape(16, 8, 128, 512).transpose(0, 2, 1, 3))
    edh = np.ascontiguousarray(ed.reshape(16, 4, 128, 1024).transpose(0, 2, 1, 3))
    suh = fm(inp["L1_ffn_shared_up"])
    sdh = fm(inp["L1_ffn_shared_down"])
    maps = []
    for c in range(8):
        b, h = c // 2, c % 2
        t0 = h * TH
        pe_, po_ = f_outs[2 * b], f_outs[2 * b + 1]
        p0 = np.ascontiguousarray(pe_["op_o"][:, :, t0:t0 + TH])
        p1 = np.ascontiguousarray(po_["op_o"][:, :, t0:t0 + TH])
        maps.append(dict(P0=p0, P1=p1, S3F=d_outs[c]["s3f_o"], BOC=boc, GWT=gwt,
                         EUH=euh, EDH=edh, SUH=suh, SDH=sdh, OH16=oh, IDT=idt,
                         ONES128=ones128, ONES1=ones1, CC=cc))
    return maps


_BUILD_CACHE = {}


def _get(name, fn):
    if name not in _BUILD_CACHE:
        _BUILD_CACHE[name] = fn()
    return _BUILD_CACHE[name]


def kernel(**inputs):
    cores = list(range(8))

    nc = _get("A", build_A)
    a_res = run_bass_kernel_spmd(nc, host_inputs_A(inputs), core_ids=cores)
    a_outs = a_res.results

    nc = _get("B", build_B)
    b_res = run_bass_kernel_spmd(nc, host_inputs_B(a_outs, inputs), core_ids=cores)
    b_outs = b_res.results

    nc = _get("C", build_C)
    c_res = run_bass_kernel_spmd(nc, host_inputs_C(b_outs, inputs), core_ids=cores)
    c_outs = c_res.results

    nc = _get("D", build_D)
    d_res = run_bass_kernel_spmd(nc, host_inputs_D(c_outs, inputs), core_ids=cores)
    d_outs = d_res.results

    nc = _get("E", build_E)
    e_res = run_bass_kernel_spmd(nc, host_inputs_E(d_outs, inputs), core_ids=cores)
    e_outs = e_res.results

    nc = _get("F", build_F)
    f_res = run_bass_kernel_spmd(nc, host_inputs_F(e_outs, inputs), core_ids=cores)
    f_outs = f_res.results

    nc = _get("G", build_G)
    g_res = run_bass_kernel_spmd(nc, host_inputs_G(f_outs, d_outs, inputs), core_ids=cores)
    g_outs = g_res.results

    out = np.empty((BATCH, T, DM), np.float32)
    for c in range(8):
        b, h = c // 2, c % 2
        t0 = h * TH
        out[b, t0:t0 + TH, :] = unfm(g_outs[c]["s5_o"]).T
    return out
